# version 19
# baseline (speedup 1.0000x reference)
import math
import numpy as np
import ml_dtypes
import concourse.bass as bass
import concourse.mybir as mybir
from concourse.bass_utils import run_bass_kernel_spmd

F32 = mybir.dt.float32
BF16 = mybir.dt.bfloat16
I32 = mybir.dt.int32
AF = mybir.ActivationFunctionType
ALU = mybir.AluOpType
BF = ml_dtypes.bfloat16

B, T, E = 2, 8192, 2048
H, D = 16, 128
CH = 128
TOK = 2048
NT = TOK // 128
NCH = TOK // CH
NC = 8
GRP = 4
I_FFN = 5632
NJ = I_FFN // 128
GU_SH = 2 * I_FFN // NC
WD_SH = 704 // NC
I_SH = WD_SH * 8
RMS_EPS = 1e-5
LN_EPS = 1e-5
FFN_EPS = 1e-6

MASK_EXP = 0x01C00000
MASK_SIGN = -2147483648
FOUR_F32 = 0x40800000
C_BIAS = 124 << 23

ENGS = ("tensor", "vector", "scalar", "gpsimd", "sync")


def _slopes16():
    start = 2.0 ** (-2.0 ** (-(math.log2(16) - 3)))
    return np.array([start ** (i + 1) for i in range(16)], dtype=np.float64)


def host_consts(core):
    s = _slopes16()
    r = np.exp(-s)
    i_ = np.arange(CH, dtype=np.float64)
    q_decay = r[:, None] ** (i_[None, :] + 1.0)
    k_decay = r[:, None] ** (CH - 1.0 - i_[None, :])
    diff = i_[:, None] - i_[None, :]
    dmask = np.where(diff >= 0, r[:, None, None] ** np.where(diff >= 0, diff, 0.0)[None], 0.0)
    r_C = r ** float(CH)

    t0 = (core % GRP) * TOK
    Dq = D // 4
    freq = (1.0 / 1024.0) ** np.linspace(0.0, 1.0, Dq)
    freq = np.concatenate([freq, np.zeros(Dq)])
    t = np.arange(t0, t0 + TOK, dtype=np.float64)
    theta = t[:, None] * freq[None, :]
    cos = np.cos(theta).T
    sin = np.sin(theta).T
    C2 = np.concatenate([cos, cos], axis=0)
    S2 = np.concatenate([sin, -sin], axis=0)

    dmT = np.zeros((128, H * 128))
    qdb = np.zeros((128, H * 128))
    for h in range(H):
        dmT[:, h * 128:(h + 1) * 128] = dmask[h].T
        qdb[:, h * 128:(h + 1) * 128] = q_decay[h][None, :]
    kd = k_decay.T.copy()

    g = core % GRP
    base = (core // GRP) * GRP
    w = np.zeros((16, 8))
    cross = r_C ** NCH
    for j in range(g):
        w[:, base + j] = cross ** (g - 1 - j)
    wcomb = np.broadcast_to(w.reshape(1, 128), (128, 128)).copy()

    return {
        "c2": np.ascontiguousarray(C2.astype(np.float32)),
        "s2": np.ascontiguousarray(S2.astype(np.float32)),
        "dmT": np.ascontiguousarray(dmT.astype(np.float32)),
        "qdb": np.ascontiguousarray(qdb.astype(np.float32)),
        "kdt": np.ascontiguousarray(kd.astype(np.float32)),
        "wcomb": np.ascontiguousarray(wcomb.astype(np.float32)),
        "ident": np.eye(128, dtype=np.float32).astype(BF),
        "onesc": np.ones((128, 1), dtype=np.float32).astype(BF),
        "onesr": np.ones((1, 128), dtype=np.float32).astype(BF),
        "r_C": r_C,
    }


NDMA = 16


class _Op:
    __slots__ = ("idx", "eng", "fn", "inc", "deps", "need_inc", "target", "tag",
                 "sem_id")

    def __init__(self, idx, eng, fn, inc, tag):
        self.idx, self.eng, self.fn, self.inc, self.tag = idx, eng, fn, inc, tag
        self.deps = set()
        self.need_inc = False
        self.target = None
        self.sem_id = eng


class Prog:
    def __init__(self):
        self.ops = []
        self.segs = {}
        self.dma_n = 0
        self.lane_last = {}

    def _dep(self, op, d, strict):
        if d is None or d is op:
            return
        if d.eng == op.eng and not strict:
            return
        op.deps.add(d)

    def add(self, eng, fn, reads=(), writes=(), strict=(), kind=None, tag=""):
        assert eng in ENGS, eng
        op = _Op(len(self.ops), eng, fn, 16 if kind == "dma" else 1, tag)
        if kind in ("dma", "cc"):
            op.need_inc = True
        if kind == "dma":
            lane = self.dma_n % NDMA
            op.sem_id = f"dma{lane}"
            self.dma_n += 1
            prev = self.lane_last.get(lane)
            if prev is not None:
                op.deps.add(prev)
            self.lane_last[lane] = op
        cands = []
        for key in list(reads) + list(strict):
            name, a, b = key
            st = key in strict
            for seg in self.segs.get(name, ()):
                if seg[0] < b and a < seg[1]:
                    cands.append((seg[2], st))
                    seg[3].append(op)
        for key in writes:
            name, a, b = key
            segs = self.segs.get(name, [])
            out = []
            for seg in segs:
                if seg[0] < b and a < seg[1]:
                    cands.append((seg[2], False))
                    for rd in seg[3]:
                        cands.append((rd, False))
                    if seg[0] < a:
                        out.append([seg[0], a, seg[2], list(seg[3])])
                    if seg[1] > b:
                        out.append([b, seg[1], seg[2], list(seg[3])])
                else:
                    out.append(seg)
            out.append([a, b, op, []])
            self.segs[name] = out
        best = {}
        for d, st in cands:
            if d is None or d is op:
                continue
            if d.eng == op.eng and d.sem_id == op.sem_id and not st:
                continue
            k = d.sem_id
            if k not in best or d.idx > best[k][0].idx:
                best[k] = (d, st)
        for d, st in best.values():
            op.deps.add(d)
        self.ops.append(op)
        return op

    def emit(self, block, sems):
        for op in self.ops:
            for d in op.deps:
                assert d.idx < op.idx
                d.need_inc = True
        cnt = {}
        for op in self.ops:
            if op.need_inc:
                cnt[op.sem_id] = cnt.get(op.sem_id, 0) + op.inc
                op.target = cnt[op.sem_id]
        per = {e: [op for op in self.ops if op.eng == e] for e in ENGS}

        def runner(eng):
            def run(e):
                hw = {}
                for op in per[eng]:
                    waits = {}
                    for d in op.deps:
                        if d.target and d.target > waits.get(d.sem_id, 0):
                            waits[d.sem_id] = d.target
                    for wsem, val in sorted(waits.items()):
                        if hw.get(wsem, 0) < val:
                            e.wait_ge(sems[wsem], val)
                            hw[wsem] = val
                    inst = op.fn(e)
                    if op.need_inc:
                        inst.then_inc(sems[op.sem_id], op.inc)
            return run

        block.tensor(runner("tensor"))
        block.vector(runner("vector"))
        block.scalar(runner("scalar"))
        block.gpsimd(runner("gpsimd"))
        block.sync(runner("sync"))


def K(name, a=0, b=1 << 40):
    return (name, a, b)


def build(debug=False, phases=None, dq_oc=11, dq_ec=16, dq_ag=True, postdump=False):
    _all = {"dq", "a", "b", "ag2", "c", "e"}
    phases = _all if phases is None else set(phases)
    nc = bass.Bass("TRN2", target_bir_lowering=False, debug=False)
    hc_r_C = host_consts(0)["r_C"]

    x_in = nc.dram_tensor("x", [TOK, E], F32, kind="ExternalInput").ap()
    wq_h = nc.dram_tensor("wq", [128, E * 16], F32, kind="ExternalInput").ap()
    wk_h = nc.dram_tensor("wk", [128, E * 16], F32, kind="ExternalInput").ap()
    wv_h = nc.dram_tensor("wv", [128, E * 16], F32, kind="ExternalInput").ap()
    wo_h = nc.dram_tensor("wo", [128, E * 16], F32, kind="ExternalInput").ap()
    rmsw_h = nc.dram_tensor("rmsw", [128, 16], F32, kind="ExternalInput").ap()
    gup_h = nc.dram_tensor("gup", [GU_SH, 256], I32, kind="ExternalInput").ap()
    gus_h = nc.dram_tensor("gus", [GU_SH, 16], mybir.dt.float16, kind="ExternalInput").ap()
    dnp_h = nc.dram_tensor("dnp", [E, WD_SH], I32, kind="ExternalInput").ap()
    dns_h = nc.dram_tensor("dns", [E, 11], mybir.dt.float16, kind="ExternalInput").ap()
    c2_h = nc.dram_tensor("c2", [128, TOK], F32, kind="ExternalInput").ap()
    s2_h = nc.dram_tensor("s2", [128, TOK], F32, kind="ExternalInput").ap()
    dmT_h = nc.dram_tensor("dmT", [128, 2048], F32, kind="ExternalInput").ap()
    qdb_h = nc.dram_tensor("qdb", [128, 2048], F32, kind="ExternalInput").ap()
    kd_h = nc.dram_tensor("kdt", [128, 16], F32, kind="ExternalInput").ap()
    wcomb_h = nc.dram_tensor("wcomb", [128, 128], F32, kind="ExternalInput").ap()
    ident_h = nc.dram_tensor("ident", [128, 128], BF16, kind="ExternalInput").ap()
    onesc_h = nc.dram_tensor("onesc", [128, 1], BF16, kind="ExternalInput").ap()
    onesr_h = nc.dram_tensor("onesr", [1, 128], BF16, kind="ExternalInput").ap()

    out_h = nc.dram_tensor("out", [TOK, E], F32, kind="ExternalOutput").ap()
    dbg = {}
    if debug:
        for nm, shp, dt in [("d_ln1T", [E, TOK], BF16), ("d_qrot", [H * 128, TOK], BF16),
                            ("d_ktm", [128, TOK], BF16), ("d_vtm", [128, TOK], BF16),
                            ("d_attnl", [H * 128, TOK], BF16),
                            ("d_attnc", [H * 128, TOK], BF16),
                            ("d_xmid", [TOK, E], F32), ("d_ln2T", [E, TOK], BF16),
                            ("d_wgu", [GU_SH, E], BF16), ("d_wd", [I_SH, E], BF16),
                            ("d_kv", [H * 128, 128], F32), ("d_in", [H * 128, 128], F32)]:
            dbg[nm] = nc.dram_tensor(nm, shp, dt, kind="ExternalOutput").ap()

    if postdump:
        for nm, shp, dt in [("p_stat", [128, 128], F32),
                            ("p_xmid", [TOK, E], F32), ("p_ln2T", [E, TOK], BF16),
                            ("p_attnc", [H * 128, TOK], BF16),
                            ("p_attnl", [H * 128, TOK], BF16)]:
            dbg[nm] = nc.dram_tensor(nm, shp, dt, kind="ExternalOutput").ap()
    qrot_d = nc.dram_tensor("qrot_d", [H * 128, TOK], BF16)
    attnl_d = nc.dram_tensor("attnl_d", [H * 128, TOK], BF16)
    attnc_d = nc.dram_tensor("attnc_d", [H * 128, TOK], BF16)
    xmid_d = nc.dram_tensor("xmid_d", [TOK, E], F32)
    ln2T_d = nc.dram_tensor("ln2T_d", [E, TOK], BF16)
    wguc_d = nc.dram_tensor("wguc_d", [GU_SH, E], BF16)
    wgug_d = nc.dram_tensor("wgug_d", [NC * GU_SH, E], BF16, addr_space="Shared")
    wdc_d = nc.dram_tensor("wdc_d", [I_SH, E], BF16)
    wdg_d = nc.dram_tensor("wdg_d", [I_FFN, E], BF16, addr_space="Shared")
    ag2i_g = [nc.dram_tensor(f"ag2i{g}", [4 * 128, 128], F32) for g in range(4)]
    ag2o_g = [nc.dram_tensor(f"ag2o{g}", [NC * 4 * 128, 128], F32,
                             addr_space="Shared") for g in range(4)]
    agb_i = nc.dram_tensor("agb_i", [128, 128], F32)
    agb_o = nc.dram_tensor("agb_o", [NC * 128, 128], F32, addr_space="Shared")

    P = Prog()
    from contextlib import ExitStack
    with ExitStack() as ctx:
        sbt = lambda n, sh, d: ctx.enter_context(nc.sbuf_tensor(n, sh, d))
        GB1 = sbt("GB1", [128, 32768], BF16)
        GB2 = sbt("GB2", [128, 16384], BF16)
        GB3 = sbt("GB3", [128, 16384], BF16)
        GB4 = sbt("GB4", [128, 12288], BF16)
        GB5 = sbt("GB5", [128, 8192], BF16)
        SM = sbt("SM", [128, 512], F32)
        CONST = sbt("CONST", [128, 640], BF16)
        ROW = sbt("ROW", [1, 4096], F32)
        ROWB = sbt("ROWB", [1, 2048], BF16)
        QTC = sbt("QTC", [128, 768], BF16)
        STATD = sbt("STATD", [128, 128], F32)
        PS = [ctx.enter_context(nc.psum_tensor(f"PS{i}", [128, 512], F32))
              for i in range(8)]
        sems = {e: ctx.enter_context(nc.semaphore("s_" + e)) for e in ENGS}
        for _i in range(NDMA):
            sems[f"dma{_i}"] = ctx.enter_context(nc.semaphore(f"s_dma{_i}"))
        block = ctx.enter_context(nc.Block())

        ident_t = CONST[:, 0:128]
        kd_t = CONST[:, 128:160].bitcast(F32)
        rmsw_t = CONST[:, 160:192].bitcast(F32)
        wcomb_t = CONST[:, 192:448].bitcast(F32)
        ones_col = CONST[:, 448:449]
        ones_row = CONST[0:1, 456:584]

        def smc(i):
            return SM[:, i:i + 1]

        cur = ["const"]

        def op(eng, fn, reads=(), writes=(), strict=(), kind=None, tag=""):
            if cur[0] != "const" and cur[0] not in phases:
                return None
            return P.add(eng, fn, reads=reads, writes=writes, strict=strict,
                         kind=kind, tag=tag)

        def dma(eng, out_ap, in_ap, reads=(), writes=(), tag=""):
            return op(eng, lambda e, o=out_ap, i=in_ap: e.dma_start(out=o, in_=i),
                      reads=reads, writes=writes, kind="dma", tag=tag)

        QK3 = 0x5F3759DF

        def rsqrt(out_ap, kout, x_ap, kx, n, tag, eps=None):
            xs = SM[:, 28:28 + n]
            kxs = K("SM", 28, 28 + n)
            r_ = SM[:, 24 - n if False else 32:32 + n]
            kr = K("SM", 32, 32 + n)
            t_ = SM[:, 36:36 + n]
            kt = K("SM", 36, 36 + n)
            if eps is not None:
                op("vector", lambda e: e.tensor_scalar(xs, x_ap, 1.0, eps,
                                                       op0=ALU.mult, op1=ALU.add),
                   strict=[kx], writes=[kxs], tag=tag + ".eps")
            else:
                op("vector", lambda e: e.tensor_scalar(xs, x_ap, 1.0, 0.0,
                                                       op0=ALU.mult, op1=ALU.add),
                   strict=[kx], writes=[kxs], tag=tag + ".cp")
            xi = xs.bitcast(I32)
            ri = r_.bitcast(I32)
            ti = t_.bitcast(I32)
            op("vector", lambda e: e.tensor_scalar(ri, xi, 9, None,
                                                   op0=ALU.logical_shift_right),
               strict=[kxs], writes=[kr], tag=tag + ".s1")
            op("vector", lambda e: e.tensor_scalar(ri, ri, 0x4B000000, None,
                                                   op0=ALU.bitwise_or),
               strict=[kr], writes=[kr], tag=tag + ".s2")
            op("vector", lambda e: e.tensor_scalar(t_, r_, 1.0, -8388608.0,
                                                   op0=ALU.mult, op1=ALU.add),
               strict=[kr], writes=[kt], tag=tag + ".s3")
            op("vector", lambda e: e.tensor_scalar(t_, t_, -1.0, float(QK3) / 256.0,
                                                   op0=ALU.mult, op1=ALU.add),
               strict=[kt], writes=[kt], tag=tag + ".s4")
            op("vector", lambda e: e.tensor_scalar(t_, t_, 1.0, 8388608.0,
                                                   op0=ALU.mult, op1=ALU.add),
               strict=[kt], writes=[kt], tag=tag + ".s5")
            op("vector", lambda e: e.tensor_scalar(ri, ti, 0x007FFFFF, None,
                                                   op0=ALU.bitwise_and),
               strict=[kt], writes=[kr], tag=tag + ".s6")
            op("vector", lambda e: e.tensor_scalar(ri, ri, 8, None,
                                                   op0=ALU.logical_shift_left),
               strict=[kr], writes=[kr], tag=tag + ".s7")
            for it in range(4):
                last = it == 3
                op("vector", lambda e: e.tensor_tensor(t_, r_, r_, ALU.mult),
                   strict=[kr], writes=[kt], tag=tag + f".t{it}")
                op("vector", lambda e: e.tensor_tensor(t_, t_, xs, ALU.mult),
                   strict=[kt, kxs], writes=[kt], tag=tag + f".u{it}")
                op("vector", lambda e: e.tensor_scalar(t_, t_, -0.5, 1.5,
                                                       op0=ALU.mult, op1=ALU.add),
                   strict=[kt], writes=[kt], tag=tag + f".w{it}")
                dst, kd = (out_ap, kout) if last else (r_, kr)
                op("vector", lambda e, dst=dst: e.tensor_tensor(dst, r_, t_, ALU.mult),
                   strict=[kr, kt], writes=[kd], tag=tag + f".r{it}")

        KCN = K("CONST")

        dma("sync", ident_t, ident_h, writes=[K("CONST", 0, 128)], tag="c.id")
        dma("sync", kd_t, kd_h, writes=[K("CONST", 128, 160)], tag="c.kd")
        dma("sync", rmsw_t, rmsw_h, writes=[K("CONST", 160, 192)], tag="c.rw")
        dma("sync", wcomb_t, wcomb_h, writes=[K("CONST", 192, 448)], tag="c.wc")
        dma("sync", ones_col, onesc_h, writes=[K("CONST", 448, 449)], tag="c.oc")
        dma("sync", ones_row, onesr_h, writes=[K("CONST", 456, 584)], tag="c.or")

        op("scalar", lambda e: e.preload_act_table(AF.Silu), tag="c.tbl")
        op("vector", lambda e: e.memset(smc(20), LN_EPS),
           writes=[K("SM", 20, 21)], tag="c.eps1")
        op("vector", lambda e: e.memset(smc(21), FFN_EPS),
           writes=[K("SM", 21, 22)], tag="c.eps2")

        c2_t = GB4[:, 0:4096].bitcast(F32)
        s2_t = GB4[:, 4096:8192].bitcast(F32)
        dm_t = GB4[:, 8192:12288].bitcast(F32)
        qd_t = GB5[:, 0:4096].bitcast(F32)
        dma("sync", c2_t, c2_h, writes=[K("GB4", 0, 4096)], tag="c.c2")
        dma("sync", s2_t, s2_h, writes=[K("GB4", 4096, 8192)], tag="c.s2")
        dma("sync", dm_t, dmT_h, writes=[K("GB4", 8192, 12288)], tag="c.dm")
        dma("sync", qd_t, qdb_h, writes=[K("GB5", 0, 4096)], tag="c.qd")

        cur[0] = "dq"
        patt = GB3[:, 0:4096].bitcast(I32)
        sgn = GB3[:, 4096:8192].bitcast(I32)
        scr_i = (8192, 12288)
        dqt = GB3[:, 12288:14336]
        dqt = GB3[:, 12288:14336]
        packed = GB3[:, 14336:14848].bitcast(I32)
        scal = GB3[:, 14848:14880].bitcast(F32)
        stage = GB2[:, 12288:16384]
        KPATT, KSGN, KSCR = K("GB3", 0, 4096), K("GB3", 4096, 8192), K("GB3", 8192, 12288)
        KDQ, KPACK, KSCAL = K("GB3", 12288, 14336), K("GB3", 14336, 14848), K("GB3", 14848, 14880)
        KSTG = K("GB2", 12288, 16384)

        def dq_tile(n_words, tag):
            nf = 8 * n_words
            for j in range(8):
                sh = 22 - 4 * j
                shop = ALU.logical_shift_left if sh >= 0 else ALU.logical_shift_right
                op("vector", lambda e, j=j, s=abs(sh), so=shop, nw=n_words:
                   e.tensor_scalar(patt[:, j:8 * nw:8], packed[:, 0:nw], s, MASK_EXP,
                                   op0=so, op1=ALU.bitwise_and),
                   reads=[KPACK], writes=[KPATT], tag=tag + ".p")
                op("vector", lambda e, j=j, nw=n_words:
                   e.tensor_scalar(patt[:, j:8 * nw:8], patt[:, j:8 * nw:8], C_BIAS,
                                   None, op0=ALU.bitwise_or),
                   reads=[KPATT], writes=[KPATT], tag=tag + ".pb")
                op("vector", lambda e, j=j, nw=n_words:
                   e.tensor_scalar(sgn[:, j:8 * nw:8], packed[:, 0:nw], 28 - 4 * j,
                                   MASK_SIGN, op0=ALU.logical_shift_left,
                                   op1=ALU.bitwise_and),
                   reads=[KPACK], writes=[KSGN], tag=tag + ".s")
                op("vector", lambda e, j=j, nw=n_words:
                   e.tensor_scalar(sgn[:, j:8 * nw:8], sgn[:, j:8 * nw:8], FOUR_F32,
                                   None, op0=ALU.bitwise_or),
                   reads=[KSGN], writes=[KSGN], tag=tag + ".sb")
            d_ap = patt[:, 0:nf].bitcast(F32)
            scr = GB3[:, scr_i[0]:scr_i[0] + 2 * nf].bitcast(F32)
            op("vector", lambda e, d=d_ap, s=scr:
               e.tensor_scalar(s, d, 2.0, 0.25, op0=ALU.mult, op1=ALU.subtract),
               reads=[KPATT], writes=[KSCR], tag=tag + ".t")
            op("vector", lambda e, d=d_ap, s=scr, nf=nf:
               e.tensor_tensor(dqt[:, 0:nf], d, s, ALU.min),
               reads=[KPATT, KSCR], writes=[KDQ], tag=tag + ".min")
            op("vector", lambda e, nf=nf:
               e.tensor_tensor(dqt[:, 0:nf], dqt[:, 0:nf],
                               sgn[:, 0:nf].bitcast(F32), ALU.mult),
               reads=[KDQ, KSGN], writes=[KDQ], tag=tag + ".sg")

        for oc in range(dq_oc):
            dma("gpsimd", packed, gup_h[oc * 128:(oc + 1) * 128, :],
                writes=[KPACK], tag=f"dq.g{oc}.ld")
            dma("gpsimd", scal, gus_h[oc * 128:(oc + 1) * 128, :],
                writes=[KSCAL], tag=f"dq.g{oc}.lds")
            dq_tile(256, f"dq.g{oc}")
            for g_ in range(16):
                op("vector", lambda e, g_=g_:
                   e.scalar_tensor_tensor(dqt[:, g_ * 128:(g_ + 1) * 128],
                                          dqt[:, g_ * 128:(g_ + 1) * 128],
                                          scal[:, g_:g_ + 1],
                                          dqt[:, g_ * 128:(g_ + 1) * 128],
                                          op0=ALU.mult, op1=ALU.bypass),
                   reads=[KDQ], strict=[KSCAL], writes=[KDQ], tag=f"dq.g{oc}.sc")
            for fc in range(16):
                b = fc % 2
                op("tensor", lambda e, fc=fc, b=b:
                   e.transpose(PS[b][:].bitcast(BF16)[:, 0:128],
                               dqt[:, fc * 128:(fc + 1) * 128], ident_t),
                   reads=[KDQ, KCN], writes=[K(f"PS{b}")], tag=f"dq.g{oc}.tp")
                op("scalar", lambda e, fc=fc, b=b:
                   e.copy(stage[:, fc * 128:(fc + 1) * 128],
                          PS[b][:].bitcast(BF16)[:, 0:128]),
                   reads=[K(f"PS{b}")], writes=[KSTG], tag=f"dq.g{oc}.ev")
            dma("sync", wguc_d.ap()[oc * 128:(oc + 1) * 128, :],
                stage[:, 0:2048],
                reads=[KSTG], writes=[K("wguc", oc, oc + 1)], tag=f"dq.g{oc}.st")

        for ec in range(dq_ec):
            dma("gpsimd", packed[:, 0:88], dnp_h[ec * 128:(ec + 1) * 128, :],
                writes=[KPACK], tag=f"dq.d{ec}.ld")
            dma("gpsimd", scal[:, 0:11], dns_h[ec * 128:(ec + 1) * 128, :],
                writes=[KSCAL], tag=f"dq.d{ec}.lds")
            dq_tile(88, f"dq.d{ec}")
            for s_ in range(11):
                op("vector", lambda e, s_=s_:
                   e.tensor_scalar(dqt[:, s_ * 64:(s_ + 1) * 64],
                                   dqt[:, s_ * 64:(s_ + 1) * 64],
                                   scal[:, s_:s_ + 1], None, op0=ALU.mult),
                   reads=[KDQ], strict=[KSCAL], writes=[KDQ], tag=f"dq.d{ec}.sc")
            for ib in range(11):
                b = ib % 2
                op("tensor", lambda e, ib=ib, b=b:
                   e.transpose(PS[b][:].bitcast(BF16)[0:64, 0:128],
                               dqt[:, ib * 64:(ib + 1) * 64], ident_t),
                   reads=[KDQ, KCN], writes=[K(f"PS{b}")], tag=f"dq.d{ec}.tp")
                op("scalar", lambda e, ib=ib, b=b:
                   e.copy(stage[0:64, ib * 128:(ib + 1) * 128],
                          PS[b][:].bitcast(BF16)[0:64, 0:128]),
                   reads=[K(f"PS{b}")], writes=[KSTG], tag=f"dq.d{ec}.ev")
            dma("sync",
                wdc_d.ap().rearrange("(b p) e -> p b e", p=64)[:, :, ec * 128:(ec + 1) * 128],
                stage.rearrange("p (b e) -> p b e", e=128)[0:64, 0:11, :],
                reads=[KSTG], writes=[K("wdc", ec, ec + 1)], tag=f"dq.d{ec}.st")

        if debug:
            dma("gpsimd", dbg["d_wgu"], wguc_d.ap(), reads=[K("wguc")], tag="dbg.wgu")
            dma("gpsimd", dbg["d_wd"], wdc_d.ap(), reads=[K("wdc")], tag="dbg.wd")

        if not dq_ag:
            cur[0] = "skip_ag"
        op("gpsimd", lambda e: e.collective_compute(
            "AllGather", ALU.bypass, replica_groups=[list(range(NC))],
            ins=[wguc_d.ap().opt()], outs=[wgug_d.ap().opt()]),
           reads=[K("wguc")], writes=[K("wgug")], kind="cc", tag="ag.wgu")
        op("gpsimd", lambda e: e.collective_compute(
            "AllGather", ALU.bypass, replica_groups=[list(range(NC))],
            ins=[wdc_d.ap().opt()], outs=[wdg_d.ap().opt()]),
           reads=[K("wdc")], writes=[K("wdg")], kind="cc", tag="ag.wd")
        if not dq_ag:
            cur[0] = "dq"

        cur[0] = "a"
        ln1T = GB1
        KLN1 = lambda fc: K("GB1", fc * 2048, (fc + 1) * 2048)
        xbuf = [GB2[:, 0:4096].bitcast(F32), GB2[:, 4096:8192].bitcast(F32)]
        KX = [K("GB2", 0, 4096), K("GB2", 4096, 8192)]
        ln1bf = GB2[:, 8192:10240]
        KLB = K("GB2", 8192, 10240)
        sqjA = GB2[:, 10240:12288]
        KSQA = K("GB2", 10240, 12288)

        for i in range(NT):
            p = i % 2
            dma("sync", xbuf[p], x_in[i * 128:(i + 1) * 128, :],
                writes=[KX[p]], tag=f"A{i}.x")
            c1, c2c = p, 2 + p
            op("vector", lambda e, p=p, c1=c1:
               e.scalar_tensor_tensor(ln1bf, xbuf[p], 1.0, xbuf[p],
                                      op0=ALU.mult, op1=ALU.bypass,
                                      accum_out=smc(c1)),
               reads=[KX[p]], writes=[KLB, K("SM", c1, c1 + 1)], tag=f"A{i}.s1")
            op("scalar", lambda e, p=p, c2c=c2c:
               e.activation(sqjA, xbuf[p], AF.Square, accum_out=smc(c2c)),
               reads=[KX[p]], writes=[KSQA, K("SM", c2c, c2c + 1)], tag=f"A{i}.s2")
            op("vector", lambda e, c1=c1:
               e.tensor_scalar(smc(4), smc(c1), 1.0 / E, None, op0=ALU.mult),
               strict=[K("SM", c1, c1 + 1)], writes=[K("SM", 4, 5)], tag=f"A{i}.m")
            op("vector", lambda e: e.tensor_tensor(smc(5), smc(4), smc(4), ALU.mult),
               strict=[K("SM", 4, 5)], writes=[K("SM", 5, 6)], tag=f"A{i}.m2")
            op("vector", lambda e, c2c=c2c:
               e.scalar_tensor_tensor(smc(14), smc(c2c), 1.0 / E, smc(5),
                                      op0=ALU.mult, op1=ALU.subtract),
               strict=[K("SM", c2c, c2c + 1), K("SM", 5, 6)],
               writes=[K("SM", 14, 15)], tag=f"A{i}.var")
            rsqrt(smc(6), K("SM", 6, 7), smc(14), K("SM", 14, 15), 1,
                  f"A{i}.rs", eps=LN_EPS)
            op("vector", lambda e:
               e.scalar_tensor_tensor(smc(7), smc(4), -1.0, smc(6),
                                      op0=ALU.mult, op1=ALU.mult),
               strict=[K("SM", 4, 5), K("SM", 6, 7)], writes=[K("SM", 7, 8)],
               tag=f"A{i}.nmr")
            op("scalar", lambda e, p=p:
               e.activation(ln1bf, xbuf[p], AF.Identity, bias=smc(7), scale=smc(6)),
               reads=[KX[p]], strict=[K("SM", 6, 7), K("SM", 7, 8)],
               writes=[KLB], tag=f"A{i}.ap")
            for fc in range(16):
                b = 4 + fc % 2
                op("tensor", lambda e, fc=fc, b=b:
                   e.transpose(PS[b][:].bitcast(BF16)[:, 0:128],
                               ln1bf[:, fc * 128:(fc + 1) * 128], ident_t),
                   reads=[KLB, KCN], writes=[K(f"PS{b}")], tag=f"A{i}.tp{fc}")
                op("vector", lambda e, fc=fc, b=b, i=i:
                   e.tensor_copy(ln1T[:, fc * 2048 + i * 128: fc * 2048 + (i + 1) * 128],
                                 PS[b][:].bitcast(BF16)[:, 0:128]),
                   reads=[K(f"PS{b}")],
                   writes=[K("GB1", fc * 2048 + i * 128, fc * 2048 + (i + 1) * 128)],
                   tag=f"A{i}.ev{fc}")
        if debug:
            dma("gpsimd", dbg["d_ln1T"].rearrange("(c p) t -> p c t", p=128),
                ln1T[:].rearrange("p (c t) -> p c t", t=2048),
                reads=[K("GB1")], tag="dbg.ln1T")

        cur[0] = "b"
        wstg = [GB5[:, 4096:6144].bitcast(F32), GB5[:, 6144:8192].bitcast(F32)]
        KWS = [K("GB5", 4096, 6144), K("GB5", 6144, 8192)]
        wbuf = [(GB2[:, 0:2048], GB2[:, 2048:4096], GB2[:, 4096:6144]),
                (GB2[:, 6144:8192], GB2[:, 8192:10240], GB2[:, 10240:12288])]
        KW = [(K("GB2", 0, 2048), K("GB2", 2048, 4096), K("GB2", 4096, 6144)),
              (K("GB2", 6144, 8192), K("GB2", 8192, 10240), K("GB2", 10240, 12288))]
        qraw = GB2[:, 12288:14336]
        KQRAW = K("GB2", 12288, 14336)
        vfm = GB2[:, 14336:16384]
        KVFM = K("GB2", 14336, 16384)
        qo_t = GB3[:, 0:2048]
        ko_t = GB3[:, 2048:4096]
        kt_t = GB3[:, 4096:6144]
        vt_t = GB3[:, 6144:8192]
        KQO, KKO, KKT, KVT = (K("GB3", 0, 2048), K("GB3", 2048, 4096),
                              K("GB3", 4096, 6144), K("GB3", 6144, 8192))
        kv_t = GB3[:, 8192:12288].bitcast(F32)
        KKV = lambda h: K("GB3", 8192 + h * 256, 8192 + (h + 1) * 256)
        al_t = [GB3[:, 12288:14336], GB3[:, 14336:16384]]
        KAL = [K("GB3", 12288, 14336), K("GB3", 14336, 16384)]
        qtc_t = QTC[:, 0:128]
        stt_t = QTC[:, 128:256]
        kvbf_t = QTC[:, 256:384]
        KQTC, KSTT, KKVB = K("QTC", 0, 128), K("QTC", 128, 256), K("QTC", 256, 384)
        KDM = K("GB4", 8192, 12288)
        KQD = K("GB5", 0, 4096)

        def proj_fm(w_t, kw, out_sb, kout, tagp):
            for tq in range(4):
                for fc in range(16):
                    op("tensor", lambda e, tq=tq, fc=fc, w_t=w_t:
                       e.matmul(PS[tq][:], w_t[:, fc * 128:(fc + 1) * 128],
                                ln1T[:, fc * 2048 + tq * 512: fc * 2048 + (tq + 1) * 512],
                                start=(fc == 0), stop=(fc == 15)),
                       reads=[kw, K("GB1")], writes=[K(f"PS{tq}")],
                       tag=f"{tagp}.mm{tq}.{fc}")
                op("scalar", lambda e, tq=tq, out_sb=out_sb:
                   e.copy(out_sb[:, tq * 512:(tq + 1) * 512], PS[tq][:]),
                   reads=[K(f"PS{tq}")], writes=[kout], tag=f"{tagp}.ev{tq}")

        def rotary(src, ksrc, dst, kdst, tmp, ktmp, tagp):
            op("vector", lambda e: e.tensor_copy(tmp[0:64, :], src[64:128, :]),
               reads=[ksrc], writes=[ktmp], tag=tagp + ".sw0")
            op("vector", lambda e: e.tensor_copy(tmp[64:128, :], src[0:64, :]),
               reads=[ksrc], writes=[ktmp], tag=tagp + ".sw1")
            op("vector", lambda e: e.tensor_tensor(dst, src, c2_t, ALU.mult),
               reads=[ksrc, K("GB4", 0, 4096)], writes=[kdst], tag=tagp + ".c")
            op("vector", lambda e: e.tensor_tensor(tmp, tmp, s2_t, ALU.mult),
               reads=[ktmp, K("GB4", 4096, 8192)], writes=[ktmp], tag=tagp + ".s")
            op("vector", lambda e: e.tensor_tensor(dst, dst, tmp, ALU.add),
               reads=[kdst, ktmp], writes=[kdst], tag=tagp + ".a")

        WSRC = [None, None, None]

        def _wparts(hh):
            p = hh % 2
            parts = []
            for wi, srcw in enumerate([wq_h, wk_h, wv_h]):
                for hv in range(2):
                    parts.append((wi, hv, (2 * wi + hv) % 2, wbuf[p][wi],
                                  KW[p][wi], srcw))
            return parts

        def emit_wload_dma(hh, j):
            wi, hv, sp, wt, kw, srcw = _wparts(hh)[j]
            dma("sync", wstg[sp],
                srcw[:, hh * 2048 + hv * 1024: hh * 2048 + (hv + 1) * 1024],
                writes=[KWS[sp]], tag=f"B{hh}.w{wi}.{hv}")

        def emit_wload_conv(hh, j):
            wi, hv, sp, wt, kw, srcw = _wparts(hh)[j]
            op("scalar", lambda e, wt=wt, hv=hv, sp=sp:
               e.copy(wt[:, hv * 1024:(hv + 1) * 1024], wstg[sp]),
               reads=[KWS[sp]], writes=[kw], tag=f"B{hh}.wc{wi}.{hv}")

        def emit_wload(hh):
            for j in range(6):
                emit_wload_dma(hh, j)
                emit_wload_conv(hh, j)

        PDEST = lambda hh: [(qraw, KQRAW), (al_t[hh % 2], KAL[hh % 2]), (vfm, KVFM)]

        def emit_proj_group(hh, wi, tq):
            p = hh % 2
            w_t, kw = wbuf[p][wi], KW[p][wi]
            dst, kdst = PDEST(hh)[wi]
            for fc in range(16):
                op("tensor", lambda e, tq=tq, fc=fc, w_t=w_t:
                   e.matmul(PS[tq][:], w_t[:, fc * 128:(fc + 1) * 128],
                            ln1T[:, fc * 2048 + tq * 512: fc * 2048 + (tq + 1) * 512],
                            start=(fc == 0), stop=(fc == 15)),
                   reads=[kw, K("GB1")], writes=[K(f"PS{tq}")],
                   tag=f"B{hh}.p{wi}.mm{tq}.{fc}")
            op("scalar", lambda e, tq=tq, dst=dst:
               e.copy(dst[:, tq * 512:(tq + 1) * 512], PS[tq][:]),
               reads=[K(f"PS{tq}")], writes=[kdst], tag=f"B{hh}.p{wi}.ev{tq}")

        def emit_pre(h, feed=lambda n: None):
            p = h % 2
            rotary(qraw, KQRAW, qo_t, KQO, kt_t, KKT, f"B{h}.qr")
            feed(16)
            dma("scalar", qrot_d.ap()[h * 128:(h + 1) * 128, :], qo_t,
                reads=[KQO], writes=[K("qrot", h, h + 1)], tag=f"B{h}.qsp")
            if debug:
                dma("gpsimd", dbg["d_qrot"][h * 128:(h + 1) * 128, :], qo_t,
                    reads=[KQO], tag=f"dbg.qrot{h}")
            rotary(al_t[p], KAL[p], ko_t, KKO, kt_t, KKT, f"B{h}.kr")
            feed(18)
            for c in range(NCH):
                b = 4 + c % 2
                op("tensor", lambda e, c=c, b=b:
                   e.transpose(PS[b][:].bitcast(BF16)[:, 0:128],
                               ko_t[:, c * 128:(c + 1) * 128], ident_t),
                   reads=[KKO, KCN], writes=[K(f"PS{b}")], tag=f"B{h}.ktp{c}")
                op("vector", lambda e, c=c, b=b, h=h:
                   e.tensor_scalar(kt_t[:, c * 128:(c + 1) * 128],
                                   PS[b][:].bitcast(BF16)[:, 0:128],
                                   kd_t[:, h:h + 1], None, op0=ALU.mult),
                   reads=[K(f"PS{b}")], strict=[K("CONST", 128, 160)],
                   writes=[KKT], tag=f"B{h}.kev{c}")
                feed(1)
            for c in range(NCH):
                b = 4 + c % 2
                op("tensor", lambda e, c=c, b=b:
                   e.transpose(PS[b][:].bitcast(BF16)[:, 0:128],
                               vfm[:, c * 128:(c + 1) * 128], ident_t),
                   reads=[KVFM, KCN], writes=[K(f"PS{b}")], tag=f"B{h}.vtp{c}")
                op("vector", lambda e, c=c, b=b:
                   e.tensor_copy(vt_t[:, c * 128:(c + 1) * 128],
                                 PS[b][:].bitcast(BF16)[:, 0:128]),
                   reads=[K(f"PS{b}")], writes=[KVT], tag=f"B{h}.vev{c}")
                feed(1)
            if debug and h == 0:
                dma("gpsimd", dbg["d_ktm"], kt_t, reads=[KKT], tag="dbg.ktm")
                dma("gpsimd", dbg["d_vtm"], vt_t, reads=[KVT], tag="dbg.vtm")
            op("vector", lambda e, h=h:
               e.tensor_tensor(kv_t[:, h * 128:(h + 1) * 128],
                               kv_t[:, h * 128:(h + 1) * 128],
                               kv_t[:, h * 128:(h + 1) * 128], ALU.subtract),
               reads=[KKV(h)], writes=[KKV(h)], tag=f"B{h}.kv0")

        def proj_items(hh):
            items = []
            p = hh % 2
            for g in range(12):
                wi, tq = g // 4, g % 4
                w_t, kw = wbuf[p][wi], KW[p][wi]
                dst, kdst = PDEST(hh)[wi]
                for fc in range(16):
                    items.append(lambda tq=tq, fc=fc, w_t=w_t, kw=kw, hh=hh, wi=wi:
                        op("tensor", lambda e, tq=tq, fc=fc, w_t=w_t:
                           e.matmul(PS[tq][:], w_t[:, fc * 128:(fc + 1) * 128],
                                    ln1T[:, fc * 2048 + tq * 512: fc * 2048 + (tq + 1) * 512],
                                    start=(fc == 0), stop=(fc == 15)),
                           reads=[kw, K("GB1")], writes=[K(f"PS{tq}")],
                           tag=f"B{hh}.p{wi}.mm{tq}.{fc}"))
                items.append(lambda tq=tq, dst=dst, kdst=kdst, hh=hh, wi=wi:
                    op("scalar", lambda e, tq=tq, dst=dst:
                       e.copy(dst[:, tq * 512:(tq + 1) * 512], PS[tq][:]),
                       reads=[K(f"PS{tq}")], writes=[kdst], tag=f"B{hh}.p{wi}.ev{tq}"))
            return items

        def emit_sc(h, c):
            sb_ = 4 + 3 * (c % 2)
            op("tensor", lambda e, c=c, sb_=sb_:
               e.matmul(PS[sb_][:, 0:128], ko_t[:, c * 128:(c + 1) * 128],
                        qo_t[:, c * 128:(c + 1) * 128], start=True, stop=True),
               reads=[KKO, KQO], writes=[K(f"PS{sb_}")], tag=f"B{h}.c{c}.sc")

        def emit_scan_rest(h, c):
            p = h % 2
            cp = c % 2
            rC = float(hc_r_C[h])
            qtc = QTC[:, cp * 128:(cp + 1) * 128]
            stt = QTC[:, 256 + cp * 128:256 + (cp + 1) * 128]
            kvb = QTC[:, 512 + cp * 128:512 + (cp + 1) * 128]
            KQTCc = K("QTC", cp * 128, (cp + 1) * 128)
            KSTTc = K("QTC", 256 + cp * 128, 256 + (cp + 1) * 128)
            KKVBc = K("QTC", 512 + cp * 128, 512 + (cp + 1) * 128)
            sb_ = 4 + 3 * cp
            op("scalar", lambda e, h=h, kvb=kvb:
               e.copy(kvb, kv_t[:, h * 128:(h + 1) * 128]),
               reads=[KKV(h)], writes=[KKVBc], tag=f"B{h}.c{c}.kvbf")
            op("vector", lambda e, c=c, h=h, qtc=qtc:
               e.tensor_tensor(qtc, qo_t[:, c * 128:(c + 1) * 128],
                               qd_t[:, h * 128:(h + 1) * 128], ALU.mult),
               reads=[KQO, KQD], writes=[KQTCc], tag=f"B{h}.c{c}.qtc")
            if c + 1 < NCH:
                emit_sc(h, c + 1)
            op("tensor", lambda e, c=c:
               e.matmul(PS[6][:, 0:128], kt_t[:, c * 128:(c + 1) * 128],
                        vt_t[:, c * 128:(c + 1) * 128], start=True, stop=True),
               reads=[KKT, KVT], writes=[K("PS6")], tag=f"B{h}.c{c}.kp")
            op("vector", lambda e, h=h, sb_=sb_, stt=stt:
               e.tensor_tensor(stt, PS[sb_][:, 0:128],
                               dm_t[:, h * 128:(h + 1) * 128], ALU.mult),
               reads=[K(f"PS{sb_}"), KDM], writes=[KSTTc], tag=f"B{h}.c{c}.st")
            op("tensor", lambda e, c=c, stt=stt:
               e.matmul(PS[5][:, 0:128], vt_t[:, c * 128:(c + 1) * 128], stt,
                        start=True, stop=False),
               reads=[KVT, KSTTc], writes=[K("PS5")], tag=f"B{h}.c{c}.o1")
            op("tensor", lambda e, kvb=kvb, qtc=qtc:
               e.matmul(PS[5][:, 0:128], kvb, qtc, start=False, stop=True),
               reads=[KKVBc, KQTCc], writes=[K("PS5")], tag=f"B{h}.c{c}.o2")
            op("vector", lambda e, h=h, rC=rC:
               e.scalar_tensor_tensor(kv_t[:, h * 128:(h + 1) * 128],
                                      kv_t[:, h * 128:(h + 1) * 128], rC,
                                      PS[6][:, 0:128], op0=ALU.mult, op1=ALU.add),
               reads=[KKV(h), K("PS6")], writes=[KKV(h)], tag=f"B{h}.c{c}.kv")
            op("scalar", lambda e, c=c, p=p:
               e.copy(al_t[p][:, c * 128:(c + 1) * 128], PS[5][:, 0:128]),
               reads=[K("PS5")], writes=[KAL[p]], tag=f"B{h}.c{c}.oev")

        emit_wload(0)
        emit_wload(1)
        for it in proj_items(0):
            it()
        for h in range(H):
            if h + 2 < H:
                for j in range(2):
                    emit_wload_dma(h + 2, j)
            stream = proj_items(h + 1) if h + 1 < H else []
            L = len(stream)
            pos = [0]

            def feed(n, pos=pos, stream=stream, L=L):
                e = min(pos[0] + n, L)
                while pos[0] < e:
                    stream[pos[0]]()
                    pos[0] += 1

            emit_pre(h, feed)
            base = pos[0]
            prev_t = base
            emit_sc(h, 0)
            for c in range(NCH):
                if h + 2 < H and c % 2 == 1 and c < 12:
                    emit_wload_conv(h + 2, c // 2)
                    if c // 2 + 2 < 6:
                        emit_wload_dma(h + 2, c // 2 + 2)
                tgt = base + (L - base) * (c + 1) // NCH
                mid = (prev_t + tgt + 1) // 2
                feed(mid - pos[0])
                emit_scan_rest(h, c)
                feed(tgt - pos[0])
                prev_t = tgt
            dma("scalar", attnl_d.ap()[h * 128:(h + 1) * 128, :], al_t[h % 2],
                reads=[KAL[h % 2]], writes=[K("attnl", h, h + 1)], tag=f"B{h}.alsp")
            if h % 4 == 3:
                g = h // 4
                dma("gpsimd", ag2i_g[g].ap().rearrange("(hh p) d -> p hh d", p=128),
                    kv_t[:, g * 512:(g + 1) * 512].rearrange(
                        "p (hh d) -> p hh d", d=128),
                    reads=[K("GB3", 8192 + g * 1024, 8192 + (g + 1) * 1024)],
                    writes=[K(f"ag2i{g}")], tag=f"ag2.in{g}")
                op("gpsimd", lambda e, g=g: e.collective_compute(
                    "AllGather", ALU.bypass, replica_groups=[list(range(NC))],
                    ins=[ag2i_g[g].ap().opt()], outs=[ag2o_g[g].ap().opt()]),
                   reads=[K(f"ag2i{g}")], writes=[K(f"ag2o{g}")], kind="cc",
                   tag=f"ag2.cc{g}")

        cur[0] = "ag2"
        if debug:
            for g in range(4):
                dma("gpsimd", dbg["d_kv"][g * 512:(g + 1) * 512, :],
                    ag2i_g[g].ap(), reads=[K(f"ag2i{g}")], tag=f"dbg.kv{g}")
        dma("gpsimd", agb_i.ap(), ag2i_g[3].ap()[0:128, :],
            reads=[K("ag2i3")], writes=[K("agbi")], tag="agb.cp")
        op("gpsimd", lambda e: e.collective_compute(
            "AllGather", ALU.bypass, replica_groups=[list(range(NC))],
            ins=[agb_i.ap().opt()], outs=[agb_o.ap().opt()]),
           reads=[K("agbi")],
           writes=[K("ag2o3"), K("wgug"), K("wdg")], kind="cc", tag="agb.cc")

        gath2 = [GB2[:, 0:2048].bitcast(F32), GB2[:, 6144:8192].bitcast(F32)]
        KGATH2 = [K("GB2", 0, 2048), K("GB2", 6144, 8192)]
        inacc = GB2[:, 2048:2304].bitcast(F32)
        KINA = K("GB2", 2048, 2304)
        inacc2 = GB2[:, 2432:2688].bitcast(F32)
        KINA2 = K("GB2", 2432, 2688)
        inbf = GB2[:, 2304:2432]
        KINB = K("GB2", 2304, 2432)
        sqc = GB2[:, 4096:6144]
        KSQC = K("GB2", 4096, 6144)
        qprep = GB2[:, 12288:14336]
        KQP = K("GB2", 12288, 14336)
        qo_s4 = [GB3[:, 0:2048], GB3[:, 2048:4096],
                 GB3[:, 8192:10240], GB3[:, 10240:12288]]
        KQOS4 = [K("GB3", 0, 2048), K("GB3", 2048, 4096),
                 K("GB3", 8192, 10240), K("GB3", 10240, 12288)]
        co_t4 = [GB3[:, 4096:6144], GB3[:, 6144:8192],
                 GB3[:, 12288:14336], GB3[:, 14336:16384]]
        KCO4 = [K("GB3", 4096, 6144), K("GB3", 6144, 8192),
                K("GB3", 12288, 14336), K("GB3", 14336, 16384)]

        def c_gath(h):
            g, hh, p2 = h // 4, h % 4, h % 2
            deps = [K(f"ag2o{g}")] + ([K(f"ag2o{g + 1}")] if g < 3 else [])
            for rh in range(2):
                dma("sync",
                    gath2[p2].rearrange("p (r d) -> p r d", d=128)
                    [:, rh * 4:(rh + 1) * 4, :],
                    ag2o_g[g].ap().rearrange("(r hh p) d -> p r hh d", r=NC, p=128)
                    [:, rh * 4:(rh + 1) * 4, hh, :],
                    reads=deps, writes=[KGATH2[p2]], tag=f"C{h}.gth{rh}")

        def c_qld(h):
            s = h % 4
            for hv in range(2):
                dma("sync", qo_s4[s][:, hv * 1024:(hv + 1) * 1024],
                    qrot_d.ap()[h * 128:(h + 1) * 128, hv * 1024:(hv + 1) * 1024],
                    reads=[K("qrot", h, h + 1)], writes=[KQOS4[s]],
                    tag=f"C{h}.qld{hv}")

        def c_ald(h):
            s = h % 4
            for hv in range(2):
                dma("sync", co_t4[s][:, hv * 1024:(hv + 1) * 1024],
                    attnl_d.ap()[h * 128:(h + 1) * 128, hv * 1024:(hv + 1) * 1024],
                    reads=[K("attnl", h, h + 1)], writes=[KCO4[s]],
                    tag=f"C{h}.ald{hv}")

        c_gath(0)
        for h0 in range(2):
            c_qld(h0)
            c_ald(h0)

        for h in range(H):
            p = h % 2
            s4 = h % 4
            if h + 1 < H:
                c_gath(h + 1)
            if h + 2 < H:
                c_qld(h + 2)
                c_ald(h + 2)
            for cg in (2 * h, 2 * h + 1):
                sp = cg % 2
                dma("sync", wstg[sp], wo_h[:, cg * 1024:(cg + 1) * 1024],
                    writes=[KWS[sp]], tag=f"W.ld{cg}")
                op("scalar", lambda e, cg=cg, sp=sp:
                   e.copy(GB1[:, cg * 1024:(cg + 1) * 1024], wstg[sp]),
                   reads=[KWS[sp]], writes=[K("GB1", cg * 1024, (cg + 1) * 1024)],
                   tag=f"W.cv{cg}")
            op("vector", lambda e, ein=h:
               e.scalar_tensor_tensor(GB1[:, ein * 2048:(ein + 1) * 2048],
                                      GB1[:, ein * 2048:(ein + 1) * 2048],
                                      rmsw_t[:, ein:ein + 1],
                                      GB1[:, ein * 2048:(ein + 1) * 2048],
                                      op0=ALU.mult, op1=ALU.bypass),
               reads=[K("GB1", h * 2048, (h + 1) * 2048)],
               strict=[K("CONST", 160, 192)],
               writes=[K("GB1", h * 2048, (h + 1) * 2048)], tag=f"W.rw{h}")
            op("vector", lambda e:
               e.tensor_tensor(inacc, inacc, inacc, ALU.subtract),
               reads=[KINA], writes=[KINA], tag=f"C{h}.inz")
            accs = [inacc, inacc2]
            kaccs = [KINA, KINA2]
            for r_ in range(0, NC):
                src_a, dst_a = accs[r_ % 2], accs[1 - r_ % 2]
                ksrc, kdst = kaccs[r_ % 2], kaccs[1 - r_ % 2]
                op("vector", lambda e, r_=r_, h=h, s_=src_a, d_=dst_a, p=p:
                   e.scalar_tensor_tensor(d_, gath2[p][:, r_ * 128:(r_ + 1) * 128],
                                          wcomb_t[:, h * 8 + r_:h * 8 + r_ + 1],
                                          s_, op0=ALU.mult, op1=ALU.add),
                   reads=[KGATH2[p], ksrc], strict=[K("CONST", 192, 448)],
                   writes=[kdst], tag=f"C{h}.in{r_}")
            op("scalar", lambda e: e.copy(inbf, inacc2),
               reads=[KINA2], writes=[KINB], tag=f"C{h}.inb")
            if debug:
                dma("gpsimd", dbg["d_in"][h * 128:(h + 1) * 128, :], inacc,
                    reads=[KINA], tag=f"dbg.in{h}")
            rC = float(hc_r_C[h])
            for l_ in range(NCH):
                op("vector", lambda e, l_=l_, s4=s4, h=h, rl=float(rC ** l_):
                   e.scalar_tensor_tensor(qprep[:, l_ * 128:(l_ + 1) * 128],
                                          qo_s4[s4][:, l_ * 128:(l_ + 1) * 128], rl,
                                          qd_t[:, h * 128:(h + 1) * 128],
                                          op0=ALU.mult, op1=ALU.mult),
                   reads=[KQOS4[s4], KQD], writes=[KQP], tag=f"C{h}.qp{l_}")
            for tq in range(4):
                op("tensor", lambda e, tq=tq:
                   e.matmul(PS[tq][:], inbf, qprep[:, tq * 512:(tq + 1) * 512],
                            start=True, stop=True),
                   reads=[KINB, KQP], writes=[K(f"PS{tq}")], tag=f"C{h}.dmm{tq}")
            for tq in range(4):
                op("vector", lambda e, tq=tq, s4=s4:
                   e.tensor_tensor(co_t4[s4][:, tq * 512:(tq + 1) * 512],
                                   co_t4[s4][:, tq * 512:(tq + 1) * 512],
                                   PS[tq][:], ALU.add),
                   reads=[KCO4[s4], K(f"PS{tq}")], writes=[KCO4[s4]],
                   tag=f"C{h}.add{tq}")
            dma("scalar", attnc_d.ap()[h * 128:(h + 1) * 128, :], co_t4[s4],
                reads=[KCO4[s4]], writes=[K("attnc", h, h + 1)], tag=f"C{h}.csp")
            op("scalar", lambda e, s4=s4: e.activation(sqc, co_t4[s4], AF.Square),
               reads=[KCO4[s4]], writes=[KSQC], tag=f"C{h}.sq")
            for tq in range(4):
                op("tensor", lambda e, tq=tq, h=h:
                   e.matmul(PS[4 + tq][0:1, :], ones_col,
                            sqc[:, tq * 512:(tq + 1) * 512],
                            start=(h == 0), stop=(h == H - 1)),
                   reads=[KSQC, KCN], writes=[K(f"PS{4 + tq}")], tag=f"C{h}.ssq{tq}")
        if debug:
            dma("gpsimd", dbg["d_attnl"], attnl_d.ap(), reads=[K("attnl")], tag="dbg.al")
            dma("gpsimd", dbg["d_attnc"], attnc_d.ap(), reads=[K("attnc")], tag="dbg.ac")

        for tq in range(4):
            op("vector", lambda e, tq=tq:
               e.tensor_scalar(ROW[0:1, tq * 512:(tq + 1) * 512], PS[4 + tq][0:1, :],
                               1.0 / E, RMS_EPS, op0=ALU.mult, op1=ALU.add),
               reads=[K(f"PS{4 + tq}")], writes=[K("ROW", tq * 512, (tq + 1) * 512)],
               tag=f"R.m{tq}")
        op("scalar", lambda e: e.copy(ROWB[0:1, :], ROW[0:1, 0:2048]),
           reads=[K("ROW", 0, 2048)], writes=[K("ROWB")], tag="R.bf")
        for i in range(NT):
            op("tensor", lambda e, i=i:
               e.matmul(PS[4][:, i:i + 1], ROWB[0:1, i * 128:(i + 1) * 128],
                        ones_row[0:1, 0:1], start=(i == 0), stop=(i == NT - 1)),
               reads=[K("ROWB"), KCN], writes=[K("PS4")], tag=f"R.c{i}")
        op("vector", lambda e: e.tensor_copy(SM[:, 48:64], PS[4][:, 0:16]),
           reads=[K("PS4")], writes=[K("SM", 48, 64)], tag="R.ev")
        for i in range(NT):
            rsqrt(SM[:, 16 + i:17 + i], K("SM", 16 + i, 17 + i),
                  SM[:, 48 + i:49 + i], K("SM", 48 + i, 49 + i), 1, f"R.rs{i}")

        cur[0] = "c"
        ant_t = [GB3[:, 4096:6144], GB3[:, 6144:8192]]
        KANT = [K("GB3", 4096, 6144), K("GB3", 6144, 8192)]
        xc_t2 = [GB3[:, 8192:12288].bitcast(F32), GB3[:, 0:4096].bitcast(F32)]
        KXC2 = [K("GB3", 8192, 12288), K("GB3", 0, 4096)]
        xmid_t = [GB5[:, 0:4096].bitcast(F32), GB5[:, 4096:8192].bitcast(F32)]
        KXM = [K("GB5", 0, 4096), K("GB5", 4096, 8192)]
        yc_t2 = [GB4[:, 8192:12288].bitcast(F32), GB2[:, 4096:8192].bitcast(F32)]
        KYC2 = [K("GB4", 8192, 12288), K("GB2", 4096, 8192)]
        lnv_t = GB4[:, 0:4096].bitcast(F32)
        KLNV = K("GB4", 0, 4096)
        sqc2 = GB4[:, 4096:6144]
        KSQ2 = K("GB4", 4096, 6144)
        ln2b2 = [GB4[:, 6144:8192], GB2[:, 0:2048]]
        KL2B2 = [K("GB4", 6144, 8192), K("GB2", 0, 2048)]
        l2s_t = [GB3[:, 12288:14336], GB3[:, 14336:16384]]
        KL2S = [K("GB3", 12288, 14336), K("GB3", 14336, 16384)]

        def emit_tp4(i, g):
            p = i % 2
            for fc in range(g * 4, g * 4 + 4):
                b = 4 + fc % 2
                op("tensor", lambda e, fc=fc, b=b, p=p:
                   e.transpose(PS[b][:].bitcast(BF16)[:, 0:128],
                               ln2b2[p][:, fc * 128:(fc + 1) * 128], ident_t),
                   reads=[KL2B2[p], KCN], writes=[K(f"PS{b}")], tag=f"D{i}.tp{fc}")
                op("vector", lambda e, fc=fc, b=b, p=p:
                   e.tensor_copy(l2s_t[p][:, fc * 128:(fc + 1) * 128],
                                 PS[b][:].bitcast(BF16)[:, 0:128]),
                   reads=[K(f"PS{b}")], writes=[KL2S[p]], tag=f"D{i}.tev{fc}")
            if g == 3:
                dma("scalar",
                    ln2T_d.ap().rearrange("(c p) t -> p c t", p=128)[:, :, i * 128:(i + 1) * 128],
                    l2s_t[p].rearrange("p (c t) -> p c t", t=128),
                    reads=[KL2S[p]], writes=[K("ln2T", i, i + 1)], tag=f"D{i}.l2sp")

        for i in range(NT):
            p = i % 2
            yc_t, KYC = yc_t2[p], KYC2[p]
            ln2b, KL2B = ln2b2[p], KL2B2[p]
            xc_t, KXC = xc_t2[p], KXC2[p]
            dma("sync", ant_t[p].rearrange("p (c t) -> p c t", t=128),
                attnc_d.ap().rearrange("(c q) t -> q c t", q=128)[:, :, i * 128:(i + 1) * 128],
                reads=[K("attnc")], writes=[KANT[p]], tag=f"D{i}.ant")
            for hv in range(2):
                dma("sync", xc_t[:, hv * 1024:(hv + 1) * 1024],
                    x_in[i * 128:(i + 1) * 128, hv * 1024:(hv + 1) * 1024],
                    writes=[KXC], tag=f"D{i}.x{hv}")
            for tq in range(4):
                for ein in range(16):
                    op("tensor", lambda e, tq=tq, ein=ein, p=p:
                       e.matmul(PS[tq][:], ant_t[p][:, ein * 128:(ein + 1) * 128],
                                GB1[:, ein * 2048 + tq * 512: ein * 2048 + (tq + 1) * 512],
                                start=(ein == 0), stop=(ein == 15)),
                       reads=[KANT[p], K("GB1", ein * 2048, (ein + 1) * 2048)],
                       writes=[K(f"PS{tq}")], tag=f"D{i}.mm{tq}.{ein}")
                if i > 0:
                    emit_tp4(i - 1, tq)
            for tq in range(4):
                op("vector", lambda e, tq=tq, i=i, yc_t=yc_t:
                   e.tensor_scalar(yc_t[:, tq * 512:(tq + 1) * 512], PS[tq][:],
                                   SM[:, 16 + i:17 + i], 0.0, op0=ALU.mult,
                                   op1=ALU.add, accum_out=smc(tq)),
                   reads=[K(f"PS{tq}")], strict=[K("SM", 16 + i, 17 + i)],
                   writes=[KYC, K("SM", tq, tq + 1)], tag=f"D{i}.ev{tq}")
            op("vector", lambda e: e.tensor_tensor(smc(0), smc(0), smc(1), ALU.add),
               strict=[K("SM", 0, 1), K("SM", 1, 2)], writes=[K("SM", 0, 1)], tag=f"D{i}.cmb1")
            op("vector", lambda e: e.tensor_tensor(smc(2), smc(2), smc(3), ALU.add),
               strict=[K("SM", 2, 3), K("SM", 3, 4)], writes=[K("SM", 2, 3)], tag=f"D{i}.cmb2")
            op("vector", lambda e: e.tensor_tensor(smc(0), smc(0), smc(2), ALU.add),
               strict=[K("SM", 0, 1), K("SM", 2, 3)], writes=[K("SM", 0, 1)], tag=f"D{i}.cmb3")
            op("scalar", lambda e, yc_t=yc_t:
               e.activation(sqc2, yc_t, AF.Square, accum_out=smc(8)),
               reads=[KYC], writes=[KSQ2, K("SM", 8, 9)], tag=f"D{i}.sq")
            op("vector", lambda e: e.tensor_scalar(smc(4), smc(0), 1.0 / E, None,
                                                   op0=ALU.mult),
               strict=[K("SM", 0, 1)], writes=[K("SM", 4, 5)], tag=f"D{i}.m")
            op("vector", lambda e: e.tensor_tensor(smc(5), smc(4), smc(4), ALU.mult),
               strict=[K("SM", 4, 5)], writes=[K("SM", 5, 6)], tag=f"D{i}.m2")
            op("vector", lambda e:
               e.scalar_tensor_tensor(smc(14), smc(8), 1.0 / E, smc(5),
                                      op0=ALU.mult, op1=ALU.subtract),
               strict=[K("SM", 8, 9), K("SM", 5, 6)], writes=[K("SM", 14, 15)],
               tag=f"D{i}.var")
            rsqrt(smc(6), K("SM", 6, 7), smc(14), K("SM", 14, 15), 1,
                  f"D{i}.rs", eps=LN_EPS)
            op("vector", lambda e:
               e.scalar_tensor_tensor(smc(7), smc(4), -1.0, smc(6),
                                      op0=ALU.mult, op1=ALU.mult),
               strict=[K("SM", 4, 5), K("SM", 6, 7)], writes=[K("SM", 7, 8)],
               tag=f"D{i}.nm")
            op("scalar", lambda e, yc_t=yc_t:
               e.activation(lnv_t, yc_t, AF.Identity, bias=smc(7), scale=smc(6)),
               reads=[KYC], strict=[K("SM", 6, 7), K("SM", 7, 8)], writes=[KLNV],
               tag=f"D{i}.ap")
            op("vector", lambda e, p=p, xc_t=xc_t:
               e.scalar_tensor_tensor(xmid_t[p], lnv_t, 1.0, xc_t,
                                      op0=ALU.mult, op1=ALU.add, accum_out=smc(9)),
               reads=[KLNV, KXC], writes=[KXM[p], K("SM", 9, 10)], tag=f"D{i}.xm")
            dma("scalar", xmid_d.ap()[i * 128:(i + 1) * 128, :], xmid_t[p],
                reads=[KXM[p]], writes=[K("xmid", i, i + 1)], tag=f"D{i}.xsp")
            op("scalar", lambda e, p=p:
               e.activation(sqc2, xmid_t[p], AF.Square, accum_out=smc(10)),
               reads=[KXM[p]], writes=[KSQ2, K("SM", 10, 11)], tag=f"D{i}.sq2")
            op("vector", lambda e: e.tensor_scalar(smc(4), smc(9), 1.0 / E, None,
                                                   op0=ALU.mult),
               strict=[K("SM", 9, 10)], writes=[K("SM", 4, 5)], tag=f"D{i}.m_2")
            op("vector", lambda e: e.tensor_tensor(smc(5), smc(4), smc(4), ALU.mult),
               strict=[K("SM", 4, 5)], writes=[K("SM", 5, 6)], tag=f"D{i}.m2_2")
            op("vector", lambda e:
               e.scalar_tensor_tensor(smc(14), smc(10), 1.0 / E, smc(5),
                                      op0=ALU.mult, op1=ALU.subtract),
               strict=[K("SM", 10, 11), K("SM", 5, 6)], writes=[K("SM", 14, 15)],
               tag=f"D{i}.var2")
            rsqrt(smc(6), K("SM", 6, 7), smc(14), K("SM", 14, 15), 1,
                  f"D{i}.rs2", eps=LN_EPS)
            op("vector", lambda e:
               e.scalar_tensor_tensor(smc(7), smc(4), -1.0, smc(6),
                                      op0=ALU.mult, op1=ALU.mult),
               strict=[K("SM", 4, 5), K("SM", 6, 7)], writes=[K("SM", 7, 8)],
               tag=f"D{i}.nm2")
            op("scalar", lambda e, p=p, ln2b=ln2b:
               e.activation(ln2b, xmid_t[p], AF.Identity, bias=smc(7), scale=smc(6)),
               reads=[KXM[p]], strict=[K("SM", 6, 7), K("SM", 7, 8)], writes=[KL2B],
               tag=f"D{i}.ap2")
        for g in range(4):
            emit_tp4(NT - 1, g)
        if debug:
            dma("gpsimd", dbg["d_xmid"], xmid_d.ap(), reads=[K("xmid")], tag="dbg.xm")
            dma("gpsimd", dbg["d_ln2T"], ln2T_d.ap(), reads=[K("ln2T")], tag="dbg.l2")

        cur[0] = "e"
        l2h = GB2
        KL2H = K("GB2")
        wgb = [GB3[:, 0:2048], GB3[:, 2048:4096]]
        KWG = [K("GB3", 0, 2048), K("GB3", 2048, 4096)]
        wub = [GB3[:, 4096:6144], GB3[:, 6144:8192]]
        KWU = [K("GB3", 4096, 6144), K("GB3", 6144, 8192)]
        wdb = [GB3[:, 8192:10240], GB3[:, 10240:12288]]
        KWD = [K("GB3", 8192, 10240), K("GB3", 10240, 12288)]
        silg = GB3[:, 12288:14336].bitcast(F32)
        KSIL = K("GB3", 12288, 14336)
        yn_t = GB5[:, 0:4096].bitcast(F32)
        KYN = K("GB5", 0, 4096)
        xmi_t = GB5[:, 4096:8192].bitcast(F32)
        KXMI = K("GB5", 4096, 8192)

        def hK(j):
            if j < 32:
                return K("GB1", j * 1024, (j + 1) * 1024)
            return K("GB4", (j - 32) * 1024, (j - 31) * 1024)

        def h_ap(j):
            if j < 32:
                return GB1[:, j * 1024:(j + 1) * 1024]
            return GB4[:, (j - 32) * 1024:(j - 31) * 1024]

        for hf in range(2):
            for fc in range(16):
                dma("sync", l2h[:, fc * 1024:(fc + 1) * 1024],
                    ln2T_d.ap()[fc * 128:(fc + 1) * 128, hf * 1024:(hf + 1) * 1024],
                    reads=[K("ln2T")], writes=[K("GB2", fc * 1024, (fc + 1) * 1024)],
                    tag=f"E{hf}.l2{fc}")
            for j in range(NJ):
                p = j % 2
                rg_, jj = j // 11, j % 11
                ru_ = rg_ + 4
                for hv in range(2):
                    dma("sync", wgb[p][:, hv * 1024:(hv + 1) * 1024],
                        wgug_d.ap()[(rg_ * 11 + jj) * 128:(rg_ * 11 + jj + 1) * 128,
                                    hv * 1024:(hv + 1) * 1024],
                        writes=[KWG[p]], tag=f"E{hf}.g{j}.wg{hv}")
                    dma("sync", wub[p][:, hv * 1024:(hv + 1) * 1024],
                        wgug_d.ap()[(ru_ * 11 + jj) * 128:(ru_ * 11 + jj + 1) * 128,
                                    hv * 1024:(hv + 1) * 1024],
                        writes=[KWU[p]], tag=f"E{hf}.g{j}.wu{hv}")
                for tq in range(2):
                    for fc in range(16):
                        op("tensor", lambda e, tq=tq, fc=fc, p=p:
                           e.matmul(PS[tq][:], wgb[p][:, fc * 128:(fc + 1) * 128],
                                    l2h[:, fc * 1024 + tq * 512: fc * 1024 + (tq + 1) * 512],
                                    start=(fc == 0), stop=(fc == 15)),
                           reads=[KWG[p], KL2H], writes=[K(f"PS{tq}")],
                           tag=f"E{hf}.g{j}.gm{tq}")
                    for fc in range(16):
                        op("tensor", lambda e, tq=tq, fc=fc, p=p:
                           e.matmul(PS[2 + tq][:], wub[p][:, fc * 128:(fc + 1) * 128],
                                    l2h[:, fc * 1024 + tq * 512: fc * 1024 + (tq + 1) * 512],
                                    start=(fc == 0), stop=(fc == 15)),
                           reads=[KWU[p], KL2H], writes=[K(f"PS{2 + tq}")],
                           tag=f"E{hf}.g{j}.um{tq}")
                for tq in range(2):
                    op("scalar", lambda e, tq=tq:
                       e.activation(silg[:, tq * 512:(tq + 1) * 512], PS[tq][:], AF.Silu),
                       reads=[K(f"PS{tq}")], writes=[KSIL], tag=f"E{hf}.g{j}.si{tq}")
                    op("vector", lambda e, tq=tq, j=j:
                       e.tensor_tensor(h_ap(j)[:, tq * 512:(tq + 1) * 512],
                                       PS[2 + tq][:], silg[:, tq * 512:(tq + 1) * 512],
                                       ALU.mult),
                       reads=[K(f"PS{2 + tq}"), KSIL], writes=[hK(j)],
                       tag=f"E{hf}.g{j}.h{tq}")
            SL = [8, 9, 10, 11, 0, 1, 2, 3, 4, 5, 6, 7]
            wdsl = lambda s: GB3[:, SL[s] * 1024:(SL[s] + 1) * 1024]
            KWDS = lambda s: K("GB3", SL[s] * 1024, (SL[s] + 1) * 1024)
            ynb = lambda ti: GB2[:, ti * 4096:(ti + 1) * 4096].bitcast(F32)
            KYNB = lambda ti: K("GB2", ti * 4096, (ti + 1) * 4096)
            sched = [(pr2, eh, j) for pr2 in range(2) for eh in range(2)
                     for j in range(NJ)]

            def wd_load(idx, hf=hf):
                pr2, eh, j = sched[idx]
                s = idx % 12
                dma("gpsimd", wdsl(s),
                    wdg_d.ap()[j * 128:(j + 1) * 128, eh * 1024:(eh + 1) * 1024],
                    writes=[KWDS(s)], tag=f"E{hf}.wd{idx}")

            DEPTH = 6
            for idx in range(DEPTH):
                wd_load(idx)
            for idx, (pr2, eh, j) in enumerate(sched):
                s = idx % 12
                for ti in range(4):
                    t_ = pr2 * 4 + ti
                    for q2 in range(2):
                        op("tensor", lambda e, ti=ti, q2=q2, t_=t_, j=j, s=s:
                           e.matmul(PS[ti * 2 + q2][:],
                                    h_ap(j)[:, t_ * 128:(t_ + 1) * 128],
                                    wdsl(s)[:, q2 * 512:(q2 + 1) * 512],
                                    start=(j == 0), stop=(j == NJ - 1)),
                           reads=[hK(j), KWDS(s)], writes=[K(f"PS{ti * 2 + q2}")],
                           tag=f"E{hf}.d{idx}.{ti}{q2}")
                if idx + DEPTH < len(sched):
                    wd_load(idx + DEPTH)
                if j != NJ - 1:
                    continue
                for ti in range(4):
                    for q2 in range(2):
                        ac = 64 + ti * 4 + eh * 2 + q2
                        op("vector", lambda e, ti=ti, q2=q2, eh=eh, ac=ac:
                           e.tensor_scalar(
                               ynb(ti)[:, eh * 1024 + q2 * 512:
                                       eh * 1024 + (q2 + 1) * 512],
                               PS[ti * 2 + q2][:], 1.0, 0.0,
                               op0=ALU.mult, op1=ALU.add, accum_out=smc(ac)),
                           reads=[K(f"PS{ti * 2 + q2}")],
                           writes=[KYNB(ti), K("SM", ac, ac + 1)],
                           tag=f"E{hf}.p{pr2}e{eh}.yc{ti}{q2}")
                if eh != 1:
                    continue
                for ti in range(4):
                    t_ = pr2 * 4 + ti
                    i_gl = hf * 8 + t_
                    a0 = 64 + ti * 4
                    op("vector", lambda e, a0=a0:
                       e.tensor_tensor(smc(0), smc(a0), smc(a0 + 1), ALU.add),
                       strict=[K("SM", a0, a0 + 1), K("SM", a0 + 1, a0 + 2)],
                       writes=[K("SM", 0, 1)], tag=f"E{hf}.v{t_}.c1")
                    op("vector", lambda e, a0=a0:
                       e.tensor_tensor(smc(2), smc(a0 + 2), smc(a0 + 3), ALU.add),
                       strict=[K("SM", a0 + 2, a0 + 3), K("SM", a0 + 3, a0 + 4)],
                       writes=[K("SM", 2, 3)], tag=f"E{hf}.v{t_}.c2")
                    op("vector", lambda e: e.tensor_tensor(smc(0), smc(0), smc(2), ALU.add),
                       strict=[K("SM", 0, 1), K("SM", 2, 3)], writes=[K("SM", 0, 1)],
                       tag=f"E{hf}.v{t_}.c3")
                    op("scalar", lambda e, ti=ti:
                       e.activation(GB3[:, 12288:14336], ynb(ti), AF.Square,
                                    accum_out=smc(8)),
                       reads=[KYNB(ti)], writes=[KSIL, K("SM", 8, 9)],
                       tag=f"E{hf}.v{t_}.sq")
                    op("vector", lambda e: e.tensor_scalar(smc(4), smc(0), 1.0 / E, None,
                                                           op0=ALU.mult),
                       strict=[K("SM", 0, 1)], writes=[K("SM", 4, 5)],
                       tag=f"E{hf}.v{t_}.m")
                    op("vector", lambda e:
                       e.tensor_scalar(smc(15), smc(8), 1.0 / E, FFN_EPS,
                                       op0=ALU.mult, op1=ALU.add),
                       strict=[K("SM", 8, 9)], writes=[K("SM", 15, 16)],
                       tag=f"E{hf}.v{t_}.me")
                    rsqrt(smc(6), K("SM", 6, 7), smc(15), K("SM", 15, 16), 1,
                          f"E{hf}.v{t_}.rs")
                    op("vector", lambda e: e.tensor_tensor(smc(5), smc(4), smc(4), ALU.mult),
                       strict=[K("SM", 4, 5)], writes=[K("SM", 5, 6)],
                       tag=f"E{hf}.v{t_}.m2")
                    op("vector", lambda e:
                       e.scalar_tensor_tensor(smc(14), smc(8), 1.0 / E, smc(5),
                                              op0=ALU.mult, op1=ALU.subtract),
                       strict=[K("SM", 8, 9), K("SM", 5, 6)], writes=[K("SM", 14, 15)],
                       tag=f"E{hf}.v{t_}.var")
                    op("vector", lambda e: e.tensor_tensor(smc(11), smc(6), smc(6), ALU.mult),
                       strict=[K("SM", 6, 7)], writes=[K("SM", 11, 12)],
                       tag=f"E{hf}.v{t_}.ir2")
                    op("vector", lambda e: e.tensor_tensor(smc(12), smc(14), smc(11), ALU.mult),
                       strict=[K("SM", 14, 15), K("SM", 11, 12)], writes=[K("SM", 12, 13)],
                       tag=f"E{hf}.v{t_}.vyn")
                    rsqrt(smc(7), K("SM", 7, 8), smc(12), K("SM", 12, 13), 1,
                          f"E{hf}.v{t_}.rs3", eps=LN_EPS)
                    op("vector", lambda e: e.tensor_tensor(smc(9), smc(6), smc(7), ALU.mult),
                       strict=[K("SM", 6, 7), K("SM", 7, 8)], writes=[K("SM", 9, 10)],
                       tag=f"E{hf}.v{t_}.sc")
                    op("vector", lambda e:
                       e.scalar_tensor_tensor(smc(10), smc(4), -1.0, smc(9),
                                              op0=ALU.mult, op1=ALU.mult),
                       strict=[K("SM", 4, 5), K("SM", 9, 10)], writes=[K("SM", 10, 11)],
                       tag=f"E{hf}.v{t_}.nm")
                    dma("sync", xmi_t, xmid_d.ap()[i_gl * 128:(i_gl + 1) * 128, :],
                        reads=[K("xmid", i_gl, i_gl + 1)], writes=[KXMI],
                        tag=f"E{hf}.v{t_}.xl")
                    op("scalar", lambda e, ti=ti:
                       e.activation(ynb(ti), ynb(ti), AF.Identity,
                                    bias=smc(10), scale=smc(9)),
                       reads=[KYNB(ti)], strict=[K("SM", 9, 10), K("SM", 10, 11)],
                       writes=[KYNB(ti)], tag=f"E{hf}.v{t_}.ap")
                    op("vector", lambda e, ti=ti:
                       e.tensor_tensor(xmi_t, xmi_t, ynb(ti), ALU.add),
                       reads=[KXMI, KYNB(ti)], writes=[KXMI], tag=f"E{hf}.v{t_}.add")
                    dma("sync", out_h[i_gl * 128:(i_gl + 1) * 128, :], xmi_t,
                        reads=[KXMI], writes=[K("outd", i_gl, i_gl + 1)],
                        tag=f"E{hf}.v{t_}.out")
                    if postdump:
                        for k_, col in enumerate((13, 9, 12, 14, 6, 7, 4, 8)):
                            op("vector", lambda e, k_=k_, col=col, i_gl=i_gl:
                               e.tensor_tensor(STATD[:, i_gl * 8 + k_:i_gl * 8 + k_ + 1],
                                               smc(col), smc(col), ALU.max),
                               strict=[K("SM", col, col + 1)],
                               writes=[K("STATD", i_gl * 8 + k_, i_gl * 8 + k_ + 1)],
                               tag=f"E{hf}.v{t_}.st{k_}")


        op("scalar", lambda e: e.copy(SM[0:1, 400:401], SM[0:1, 400:401]),
           reads=[K("outd", 0, NT)], tag="fence")
        if postdump:
            dma("gpsimd", dbg["p_stat"], STATD[:],
                reads=[K("STATD")], tag="pd.stat")
            dma("gpsimd", dbg["p_xmid"], xmid_d.ap(),
                reads=[K("outd", 0, NT), K("xmid")], tag="pd.xmid")
            dma("gpsimd", dbg["p_ln2T"], ln2T_d.ap(),
                reads=[K("outd", 0, NT), K("ln2T")], tag="pd.ln2T")
            dma("gpsimd", dbg["p_attnc"], attnc_d.ap(),
                reads=[K("outd", 0, NT), K("attnc")], tag="pd.attnc")
            dma("gpsimd", dbg["p_attnl"], attnl_d.ap(),
                reads=[K("outd", 0, NT), K("attnl")], tag="pd.attnl")

        P.emit(block, sems)
    return nc, dbg


def _mk_inmap(core, inputs):
    hc = host_consts(core)
    b, s = core // GRP, core % GRP
    x = np.ascontiguousarray(inputs["L0_attn_x"][b, s * TOK:(s + 1) * TOK, :])
    gus_t = np.ascontiguousarray(
        np.asarray(inputs["L0_ffn_gate_up_scales"]).T
        [core * GU_SH:(core + 1) * GU_SH, :])
    dsc = np.asarray(inputs["L0_ffn_down_scales"])
    segs = [(core * I_SH + 64 * s_) // 128 for s_ in range(11)]
    dns = np.ascontiguousarray(dsc[segs, :].T)
    rmsw = np.ascontiguousarray(
        np.asarray(inputs["L0_attn_rms_w"], dtype=np.float32).reshape(16, 128).T)
    def _wlay(w):
        return np.ascontiguousarray(
            np.asarray(w).reshape(16, 128, 16, 128).transpose(1, 2, 0, 3)
            .reshape(128, 16 * 2048))

    def _wolay(w):
        return np.ascontiguousarray(
            np.asarray(w).reshape(16, 128, 2048).transpose(1, 0, 2)
            .reshape(128, 16 * 2048))

    return {
        "x": x,
        "wq": _wlay(inputs["L0_attn_Wq"]),
        "wk": _wlay(inputs["L0_attn_Wk"]),
        "wv": _wlay(inputs["L0_attn_Wv"]),
        "wo": _wolay(inputs["L0_attn_Wo"]),
        "rmsw": rmsw,
        "gup": np.ascontiguousarray(
            np.asarray(inputs["L0_ffn_gate_up_packed"])[core * GU_SH:(core + 1) * GU_SH, :]),
        "gus": gus_t,
        "dnp": np.ascontiguousarray(
            np.asarray(inputs["L0_ffn_down_packed"])[:, core * WD_SH:(core + 1) * WD_SH]),
        "dns": dns,
        "c2": hc["c2"], "s2": hc["s2"], "dmT": hc["dmT"], "qdb": hc["qdb"],
        "kdt": hc["kdt"], "wcomb": hc["wcomb"], "ident": hc["ident"],
        "onesc": hc["onesc"], "onesr": hc["onesr"],
    }


def kernel(**inputs):
    nc, _ = build(debug=False)
    in_maps = [_mk_inmap(c, inputs) for c in range(NC)]
    res = run_bass_kernel_spmd(nc, in_maps, core_ids=list(range(NC)))
    out = np.zeros((B, T, E), dtype=np.float32)
    for c in range(NC):
        b, s = c // GRP, c % GRP
        out[b, s * TOK:(s + 1) * TOK, :] = res.results[c]["out"]
    return out


# version 20 (final)
# speedup vs baseline: 1.0237x; 1.0237x over previous
import math
import numpy as np
import ml_dtypes
import concourse.bass as bass
import concourse.mybir as mybir
from concourse.bass_utils import run_bass_kernel_spmd

F32 = mybir.dt.float32
BF16 = mybir.dt.bfloat16
I32 = mybir.dt.int32
AF = mybir.ActivationFunctionType
ALU = mybir.AluOpType
BF = ml_dtypes.bfloat16

B, T, E = 2, 8192, 2048
H, D = 16, 128
CH = 128
TOK = 2048
NT = TOK // 128
NCH = TOK // CH
NC = 8
GRP = 4
I_FFN = 5632
NJ = I_FFN // 128
GU_SH = 2 * I_FFN // NC
WD_SH = 704 // NC
I_SH = WD_SH * 8
RMS_EPS = 1e-5
LN_EPS = 1e-5
FFN_EPS = 1e-6

MASK_EXP = 0x01C00000
MASK_SIGN = -2147483648
FOUR_F32 = 0x40800000
C_BIAS = 124 << 23

ENGS = ("tensor", "vector", "scalar", "gpsimd", "sync")


def _slopes16():
    start = 2.0 ** (-2.0 ** (-(math.log2(16) - 3)))
    return np.array([start ** (i + 1) for i in range(16)], dtype=np.float64)


def host_consts(core):
    s = _slopes16()
    r = np.exp(-s)
    i_ = np.arange(CH, dtype=np.float64)
    q_decay = r[:, None] ** (i_[None, :] + 1.0)
    k_decay = r[:, None] ** (CH - 1.0 - i_[None, :])
    diff = i_[:, None] - i_[None, :]
    dmask = np.where(diff >= 0, r[:, None, None] ** np.where(diff >= 0, diff, 0.0)[None], 0.0)
    r_C = r ** float(CH)

    t0 = (core % GRP) * TOK
    Dq = D // 4
    freq = (1.0 / 1024.0) ** np.linspace(0.0, 1.0, Dq)
    freq = np.concatenate([freq, np.zeros(Dq)])
    t = np.arange(t0, t0 + TOK, dtype=np.float64)
    theta = t[:, None] * freq[None, :]
    cos = np.cos(theta).T
    sin = np.sin(theta).T
    C2 = np.concatenate([cos, cos], axis=0)
    S2 = np.concatenate([sin, -sin], axis=0)

    dmT = np.zeros((128, H * 128))
    qdb = np.zeros((128, H * 128))
    for h in range(H):
        dmT[:, h * 128:(h + 1) * 128] = dmask[h].T
        qdb[:, h * 128:(h + 1) * 128] = q_decay[h][None, :]
    kd = k_decay.T.copy()

    g = core % GRP
    base = (core // GRP) * GRP
    w = np.zeros((16, 8))
    cross = r_C ** NCH
    for j in range(g):
        w[:, base + j] = cross ** (g - 1 - j)
    wcomb = np.broadcast_to(w.reshape(1, 128), (128, 128)).copy()

    return {
        "c2": np.ascontiguousarray(C2.astype(np.float32)),
        "s2": np.ascontiguousarray(S2.astype(np.float32)),
        "dmT": np.ascontiguousarray(dmT.astype(np.float32)),
        "qdb": np.ascontiguousarray(qdb.astype(np.float32)),
        "kdt": np.ascontiguousarray(kd.astype(np.float32)),
        "wcomb": np.ascontiguousarray(wcomb.astype(np.float32)),
        "ident": np.eye(128, dtype=np.float32).astype(BF),
        "onesc": np.ones((128, 1), dtype=np.float32).astype(BF),
        "onesr": np.ones((1, 128), dtype=np.float32).astype(BF),
        "r_C": r_C,
    }


NDMA = 16


class _Op:
    __slots__ = ("idx", "eng", "fn", "inc", "deps", "need_inc", "target", "tag",
                 "sem_id")

    def __init__(self, idx, eng, fn, inc, tag):
        self.idx, self.eng, self.fn, self.inc, self.tag = idx, eng, fn, inc, tag
        self.deps = set()
        self.need_inc = False
        self.target = None
        self.sem_id = eng


class Prog:
    def __init__(self):
        self.ops = []
        self.segs = {}
        self.dma_n = 0
        self.lane_last = {}

    def _dep(self, op, d, strict):
        if d is None or d is op:
            return
        if d.eng == op.eng and not strict:
            return
        op.deps.add(d)

    def add(self, eng, fn, reads=(), writes=(), strict=(), kind=None, tag=""):
        assert eng in ENGS, eng
        op = _Op(len(self.ops), eng, fn, 16 if kind == "dma" else 1, tag)
        if kind in ("dma", "cc"):
            op.need_inc = True
        if kind == "dma":
            lane = self.dma_n % NDMA
            op.sem_id = f"dma{lane}"
            self.dma_n += 1
            prev = self.lane_last.get(lane)
            if prev is not None:
                op.deps.add(prev)
            self.lane_last[lane] = op
        cands = []
        for key in list(reads) + list(strict):
            name, a, b = key
            st = key in strict
            for seg in self.segs.get(name, ()):
                if seg[0] < b and a < seg[1]:
                    cands.append((seg[2], st))
                    seg[3].append(op)
        for key in writes:
            name, a, b = key
            segs = self.segs.get(name, [])
            out = []
            for seg in segs:
                if seg[0] < b and a < seg[1]:
                    cands.append((seg[2], False))
                    for rd in seg[3]:
                        cands.append((rd, False))
                    if seg[0] < a:
                        out.append([seg[0], a, seg[2], list(seg[3])])
                    if seg[1] > b:
                        out.append([b, seg[1], seg[2], list(seg[3])])
                else:
                    out.append(seg)
            out.append([a, b, op, []])
            self.segs[name] = out
        best = {}
        for d, st in cands:
            if d is None or d is op:
                continue
            if d.eng == op.eng and d.sem_id == op.sem_id and not st:
                continue
            k = d.sem_id
            if k not in best or d.idx > best[k][0].idx:
                best[k] = (d, st)
        for d, st in best.values():
            op.deps.add(d)
        self.ops.append(op)
        return op

    def emit(self, block, sems):
        for op in self.ops:
            for d in op.deps:
                assert d.idx < op.idx
                d.need_inc = True
        cnt = {}
        for op in self.ops:
            if op.need_inc:
                cnt[op.sem_id] = cnt.get(op.sem_id, 0) + op.inc
                op.target = cnt[op.sem_id]
        per = {e: [op for op in self.ops if op.eng == e] for e in ENGS}

        def runner(eng):
            def run(e):
                hw = {}
                for op in per[eng]:
                    waits = {}
                    for d in op.deps:
                        if d.target and d.target > waits.get(d.sem_id, 0):
                            waits[d.sem_id] = d.target
                    for wsem, val in sorted(waits.items()):
                        if hw.get(wsem, 0) < val:
                            e.wait_ge(sems[wsem], val)
                            hw[wsem] = val
                    inst = op.fn(e)
                    if op.need_inc:
                        inst.then_inc(sems[op.sem_id], op.inc)
            return run

        block.tensor(runner("tensor"))
        block.vector(runner("vector"))
        block.scalar(runner("scalar"))
        block.gpsimd(runner("gpsimd"))
        block.sync(runner("sync"))


def K(name, a=0, b=1 << 40):
    return (name, a, b)


def build(debug=False, phases=None, dq_oc=11, dq_ec=16, dq_ag=True, postdump=False):
    _all = {"dq", "a", "b", "ag2", "c", "e"}
    phases = _all if phases is None else set(phases)
    nc = bass.Bass("TRN2", target_bir_lowering=False, debug=False)
    hc_r_C = host_consts(0)["r_C"]

    x_in = nc.dram_tensor("x", [TOK, E], F32, kind="ExternalInput").ap()
    wq_h = nc.dram_tensor("wq", [128, E * 16], F32, kind="ExternalInput").ap()
    wk_h = nc.dram_tensor("wk", [128, E * 16], F32, kind="ExternalInput").ap()
    wv_h = nc.dram_tensor("wv", [128, E * 16], F32, kind="ExternalInput").ap()
    wo_h = nc.dram_tensor("wo", [128, E * 16], F32, kind="ExternalInput").ap()
    rmsw_h = nc.dram_tensor("rmsw", [128, 16], F32, kind="ExternalInput").ap()
    gup_h = nc.dram_tensor("gup", [GU_SH, 256], I32, kind="ExternalInput").ap()
    gus_h = nc.dram_tensor("gus", [GU_SH, 16], mybir.dt.float16, kind="ExternalInput").ap()
    dnp_h = nc.dram_tensor("dnp", [E, WD_SH], I32, kind="ExternalInput").ap()
    dns_h = nc.dram_tensor("dns", [E, 11], mybir.dt.float16, kind="ExternalInput").ap()
    c2_h = nc.dram_tensor("c2", [128, TOK], F32, kind="ExternalInput").ap()
    s2_h = nc.dram_tensor("s2", [128, TOK], F32, kind="ExternalInput").ap()
    dmT_h = nc.dram_tensor("dmT", [128, 2048], F32, kind="ExternalInput").ap()
    qdb_h = nc.dram_tensor("qdb", [128, 2048], F32, kind="ExternalInput").ap()
    kd_h = nc.dram_tensor("kdt", [128, 16], F32, kind="ExternalInput").ap()
    wcomb_h = nc.dram_tensor("wcomb", [128, 128], F32, kind="ExternalInput").ap()
    ident_h = nc.dram_tensor("ident", [128, 128], BF16, kind="ExternalInput").ap()
    onesc_h = nc.dram_tensor("onesc", [128, 1], BF16, kind="ExternalInput").ap()
    onesr_h = nc.dram_tensor("onesr", [1, 128], BF16, kind="ExternalInput").ap()

    out_h = nc.dram_tensor("out", [TOK, E], F32, kind="ExternalOutput").ap()
    dbg = {}
    if debug:
        for nm, shp, dt in [("d_ln1T", [E, TOK], BF16), ("d_qrot", [H * 128, TOK], BF16),
                            ("d_ktm", [128, TOK], BF16), ("d_vtm", [128, TOK], BF16),
                            ("d_attnl", [H * 128, TOK], BF16),
                            ("d_attnc", [H * 128, TOK], BF16),
                            ("d_xmid", [TOK, E], F32), ("d_ln2T", [E, TOK], BF16),
                            ("d_wgu", [GU_SH, E], BF16), ("d_wd", [I_SH, E], BF16),
                            ("d_kv", [H * 128, 128], F32), ("d_in", [H * 128, 128], F32)]:
            dbg[nm] = nc.dram_tensor(nm, shp, dt, kind="ExternalOutput").ap()

    if postdump:
        for nm, shp, dt in [("p_stat", [128, 128], F32),
                            ("p_xmid", [TOK, E], F32), ("p_ln2T", [E, TOK], BF16),
                            ("p_attnc", [H * 128, TOK], BF16),
                            ("p_attnl", [H * 128, TOK], BF16)]:
            dbg[nm] = nc.dram_tensor(nm, shp, dt, kind="ExternalOutput").ap()
    qrot_d = nc.dram_tensor("qrot_d", [H * 128, TOK], BF16)
    attnl_d = nc.dram_tensor("attnl_d", [H * 128, TOK], BF16)
    attnc_d = nc.dram_tensor("attnc_d", [H * 128, TOK], BF16)
    xmid_d = nc.dram_tensor("xmid_d", [TOK, E], F32)
    ln2T_d = nc.dram_tensor("ln2T_d", [E, TOK], BF16)
    wguc_d = nc.dram_tensor("wguc_d", [GU_SH, E], BF16)
    wgug_d = nc.dram_tensor("wgug_d", [NC * GU_SH, E], BF16, addr_space="Shared")
    wdc_d = nc.dram_tensor("wdc_d", [I_SH, E], BF16)
    wdg_d = nc.dram_tensor("wdg_d", [I_FFN, E], BF16, addr_space="Shared")
    ag2i_g = [nc.dram_tensor(f"ag2i{g}", [4 * 128, 128], F32) for g in range(4)]
    ag2o_g = [nc.dram_tensor(f"ag2o{g}", [NC * 4 * 128, 128], F32,
                             addr_space="Shared") for g in range(4)]
    agb_i = nc.dram_tensor("agb_i", [128, 128], F32)
    agb_o = nc.dram_tensor("agb_o", [NC * 128, 128], F32, addr_space="Shared")

    P = Prog()
    from contextlib import ExitStack
    with ExitStack() as ctx:
        sbt = lambda n, sh, d: ctx.enter_context(nc.sbuf_tensor(n, sh, d))
        GB1 = sbt("GB1", [128, 32768], BF16)
        GB2 = sbt("GB2", [128, 16384], BF16)
        GB3 = sbt("GB3", [128, 16384], BF16)
        GB4 = sbt("GB4", [128, 12288], BF16)
        GB5 = sbt("GB5", [128, 8192], BF16)
        SM = sbt("SM", [128, 512], F32)
        CONST = sbt("CONST", [128, 640], BF16)
        ROW = sbt("ROW", [1, 4096], F32)
        ROWB = sbt("ROWB", [1, 2048], BF16)
        QTC = sbt("QTC", [128, 768], BF16)
        STATD = sbt("STATD", [128, 128], F32)
        PS = [ctx.enter_context(nc.psum_tensor(f"PS{i}", [128, 512], F32))
              for i in range(8)]
        sems = {e: ctx.enter_context(nc.semaphore("s_" + e)) for e in ENGS}
        for _i in range(NDMA):
            sems[f"dma{_i}"] = ctx.enter_context(nc.semaphore(f"s_dma{_i}"))
        block = ctx.enter_context(nc.Block())

        ident_t = CONST[:, 0:128]
        kd_t = CONST[:, 128:160].bitcast(F32)
        rmsw_t = CONST[:, 160:192].bitcast(F32)
        wcomb_t = CONST[:, 192:448].bitcast(F32)
        ones_col = CONST[:, 448:449]
        ones_row = CONST[0:1, 456:584]

        def smc(i):
            return SM[:, i:i + 1]

        cur = ["const"]

        def op(eng, fn, reads=(), writes=(), strict=(), kind=None, tag=""):
            if cur[0] != "const" and cur[0] not in phases:
                return None
            return P.add(eng, fn, reads=reads, writes=writes, strict=strict,
                         kind=kind, tag=tag)

        def dma(eng, out_ap, in_ap, reads=(), writes=(), tag=""):
            return op(eng, lambda e, o=out_ap, i=in_ap: e.dma_start(out=o, in_=i),
                      reads=reads, writes=writes, kind="dma", tag=tag)

        QK3 = 0x5F3759DF

        def rsqrt(out_ap, kout, x_ap, kx, n, tag, eps=None):
            xs = SM[:, 28:28 + n]
            kxs = K("SM", 28, 28 + n)
            r_ = SM[:, 24 - n if False else 32:32 + n]
            kr = K("SM", 32, 32 + n)
            t_ = SM[:, 36:36 + n]
            kt = K("SM", 36, 36 + n)
            if eps is not None:
                op("vector", lambda e: e.tensor_scalar(xs, x_ap, 1.0, eps,
                                                       op0=ALU.mult, op1=ALU.add),
                   strict=[kx], writes=[kxs], tag=tag + ".eps")
            else:
                op("vector", lambda e: e.tensor_scalar(xs, x_ap, 1.0, 0.0,
                                                       op0=ALU.mult, op1=ALU.add),
                   strict=[kx], writes=[kxs], tag=tag + ".cp")
            xi = xs.bitcast(I32)
            ri = r_.bitcast(I32)
            ti = t_.bitcast(I32)
            op("vector", lambda e: e.tensor_scalar(ri, xi, 9, None,
                                                   op0=ALU.logical_shift_right),
               strict=[kxs], writes=[kr], tag=tag + ".s1")
            op("vector", lambda e: e.tensor_scalar(ri, ri, 0x4B000000, None,
                                                   op0=ALU.bitwise_or),
               strict=[kr], writes=[kr], tag=tag + ".s2")
            op("vector", lambda e: e.tensor_scalar(t_, r_, 1.0, -8388608.0,
                                                   op0=ALU.mult, op1=ALU.add),
               strict=[kr], writes=[kt], tag=tag + ".s3")
            op("vector", lambda e: e.tensor_scalar(t_, t_, -1.0, float(QK3) / 256.0,
                                                   op0=ALU.mult, op1=ALU.add),
               strict=[kt], writes=[kt], tag=tag + ".s4")
            op("vector", lambda e: e.tensor_scalar(t_, t_, 1.0, 8388608.0,
                                                   op0=ALU.mult, op1=ALU.add),
               strict=[kt], writes=[kt], tag=tag + ".s5")
            op("vector", lambda e: e.tensor_scalar(ri, ti, 0x007FFFFF, None,
                                                   op0=ALU.bitwise_and),
               strict=[kt], writes=[kr], tag=tag + ".s6")
            op("vector", lambda e: e.tensor_scalar(ri, ri, 8, None,
                                                   op0=ALU.logical_shift_left),
               strict=[kr], writes=[kr], tag=tag + ".s7")
            for it in range(4):
                last = it == 3
                op("vector", lambda e: e.tensor_tensor(t_, r_, r_, ALU.mult),
                   strict=[kr], writes=[kt], tag=tag + f".t{it}")
                op("vector", lambda e: e.tensor_tensor(t_, t_, xs, ALU.mult),
                   strict=[kt, kxs], writes=[kt], tag=tag + f".u{it}")
                op("vector", lambda e: e.tensor_scalar(t_, t_, -0.5, 1.5,
                                                       op0=ALU.mult, op1=ALU.add),
                   strict=[kt], writes=[kt], tag=tag + f".w{it}")
                dst, kd = (out_ap, kout) if last else (r_, kr)
                op("vector", lambda e, dst=dst: e.tensor_tensor(dst, r_, t_, ALU.mult),
                   strict=[kr, kt], writes=[kd], tag=tag + f".r{it}")

        KCN = K("CONST")

        dma("sync", ident_t, ident_h, writes=[K("CONST", 0, 128)], tag="c.id")
        dma("sync", kd_t, kd_h, writes=[K("CONST", 128, 160)], tag="c.kd")
        dma("sync", rmsw_t, rmsw_h, writes=[K("CONST", 160, 192)], tag="c.rw")
        dma("sync", wcomb_t, wcomb_h, writes=[K("CONST", 192, 448)], tag="c.wc")
        dma("sync", ones_col, onesc_h, writes=[K("CONST", 448, 449)], tag="c.oc")
        dma("sync", ones_row, onesr_h, writes=[K("CONST", 456, 584)], tag="c.or")

        op("scalar", lambda e: e.preload_act_table(AF.Silu), tag="c.tbl")
        op("vector", lambda e: e.memset(smc(20), LN_EPS),
           writes=[K("SM", 20, 21)], tag="c.eps1")
        op("vector", lambda e: e.memset(smc(21), FFN_EPS),
           writes=[K("SM", 21, 22)], tag="c.eps2")

        c2_t = GB4[:, 0:4096].bitcast(F32)
        s2_t = GB4[:, 4096:8192].bitcast(F32)
        dm_t = GB4[:, 8192:12288].bitcast(F32)
        qd_t = GB5[:, 0:4096].bitcast(F32)
        dma("sync", c2_t, c2_h, writes=[K("GB4", 0, 4096)], tag="c.c2")
        dma("sync", s2_t, s2_h, writes=[K("GB4", 4096, 8192)], tag="c.s2")
        dma("sync", dm_t, dmT_h, writes=[K("GB4", 8192, 12288)], tag="c.dm")
        dma("sync", qd_t, qdb_h, writes=[K("GB5", 0, 4096)], tag="c.qd")

        cur[0] = "dq"
        patt = GB3[:, 0:4096].bitcast(I32)
        sgn = GB3[:, 4096:8192].bitcast(I32)
        scr_i = (8192, 12288)
        dqt = GB3[:, 12288:14336]
        dqt = GB3[:, 12288:14336]
        packed = GB3[:, 14336:14848].bitcast(I32)
        scal = GB3[:, 14848:14880].bitcast(F32)
        stage = GB2[:, 12288:16384]
        KPATT, KSGN, KSCR = K("GB3", 0, 4096), K("GB3", 4096, 8192), K("GB3", 8192, 12288)
        KDQ, KPACK, KSCAL = K("GB3", 12288, 14336), K("GB3", 14336, 14848), K("GB3", 14848, 14880)
        KSTG = K("GB2", 12288, 16384)

        def dq_tile(n_words, tag):
            nf = 8 * n_words
            for j in range(8):
                sh = 22 - 4 * j
                shop = ALU.logical_shift_left if sh >= 0 else ALU.logical_shift_right
                op("vector", lambda e, j=j, s=abs(sh), so=shop, nw=n_words:
                   e.tensor_scalar(patt[:, j:8 * nw:8], packed[:, 0:nw], s, MASK_EXP,
                                   op0=so, op1=ALU.bitwise_and),
                   reads=[KPACK], writes=[KPATT], tag=tag + ".p")
                op("vector", lambda e, j=j, nw=n_words:
                   e.tensor_scalar(patt[:, j:8 * nw:8], patt[:, j:8 * nw:8], C_BIAS,
                                   None, op0=ALU.bitwise_or),
                   reads=[KPATT], writes=[KPATT], tag=tag + ".pb")
                op("vector", lambda e, j=j, nw=n_words:
                   e.tensor_scalar(sgn[:, j:8 * nw:8], packed[:, 0:nw], 28 - 4 * j,
                                   MASK_SIGN, op0=ALU.logical_shift_left,
                                   op1=ALU.bitwise_and),
                   reads=[KPACK], writes=[KSGN], tag=tag + ".s")
                op("vector", lambda e, j=j, nw=n_words:
                   e.tensor_scalar(sgn[:, j:8 * nw:8], sgn[:, j:8 * nw:8], FOUR_F32,
                                   None, op0=ALU.bitwise_or),
                   reads=[KSGN], writes=[KSGN], tag=tag + ".sb")
            d_ap = patt[:, 0:nf].bitcast(F32)
            scr = GB3[:, scr_i[0]:scr_i[0] + 2 * nf].bitcast(F32)
            op("vector", lambda e, d=d_ap, s=scr:
               e.tensor_scalar(s, d, 2.0, 0.25, op0=ALU.mult, op1=ALU.subtract),
               reads=[KPATT], writes=[KSCR], tag=tag + ".t")
            op("vector", lambda e, d=d_ap, s=scr, nf=nf:
               e.tensor_tensor(dqt[:, 0:nf], d, s, ALU.min),
               reads=[KPATT, KSCR], writes=[KDQ], tag=tag + ".min")
            op("vector", lambda e, nf=nf:
               e.tensor_tensor(dqt[:, 0:nf], dqt[:, 0:nf],
                               sgn[:, 0:nf].bitcast(F32), ALU.mult),
               reads=[KDQ, KSGN], writes=[KDQ], tag=tag + ".sg")

        for oc in range(dq_oc):
            dma("gpsimd", packed, gup_h[oc * 128:(oc + 1) * 128, :],
                writes=[KPACK], tag=f"dq.g{oc}.ld")
            dma("gpsimd", scal, gus_h[oc * 128:(oc + 1) * 128, :],
                writes=[KSCAL], tag=f"dq.g{oc}.lds")
            dq_tile(256, f"dq.g{oc}")
            for g_ in range(16):
                op("vector", lambda e, g_=g_:
                   e.scalar_tensor_tensor(dqt[:, g_ * 128:(g_ + 1) * 128],
                                          dqt[:, g_ * 128:(g_ + 1) * 128],
                                          scal[:, g_:g_ + 1],
                                          dqt[:, g_ * 128:(g_ + 1) * 128],
                                          op0=ALU.mult, op1=ALU.bypass),
                   reads=[KDQ], strict=[KSCAL], writes=[KDQ], tag=f"dq.g{oc}.sc")
            for fc in range(16):
                b = fc % 2
                op("tensor", lambda e, fc=fc, b=b:
                   e.transpose(PS[b][:].bitcast(BF16)[:, 0:128],
                               dqt[:, fc * 128:(fc + 1) * 128], ident_t),
                   reads=[KDQ, KCN], writes=[K(f"PS{b}")], tag=f"dq.g{oc}.tp")
                op("scalar", lambda e, fc=fc, b=b:
                   e.copy(stage[:, fc * 128:(fc + 1) * 128],
                          PS[b][:].bitcast(BF16)[:, 0:128]),
                   reads=[K(f"PS{b}")], writes=[KSTG], tag=f"dq.g{oc}.ev")
            dma("sync", wguc_d.ap()[oc * 128:(oc + 1) * 128, :],
                stage[:, 0:2048],
                reads=[KSTG], writes=[K("wguc", oc, oc + 1)], tag=f"dq.g{oc}.st")

        for ec in range(dq_ec):
            dma("gpsimd", packed[:, 0:88], dnp_h[ec * 128:(ec + 1) * 128, :],
                writes=[KPACK], tag=f"dq.d{ec}.ld")
            dma("gpsimd", scal[:, 0:11], dns_h[ec * 128:(ec + 1) * 128, :],
                writes=[KSCAL], tag=f"dq.d{ec}.lds")
            dq_tile(88, f"dq.d{ec}")
            for s_ in range(11):
                op("vector", lambda e, s_=s_:
                   e.tensor_scalar(dqt[:, s_ * 64:(s_ + 1) * 64],
                                   dqt[:, s_ * 64:(s_ + 1) * 64],
                                   scal[:, s_:s_ + 1], None, op0=ALU.mult),
                   reads=[KDQ], strict=[KSCAL], writes=[KDQ], tag=f"dq.d{ec}.sc")
            for ib in range(11):
                b = ib % 2
                op("tensor", lambda e, ib=ib, b=b:
                   e.transpose(PS[b][:].bitcast(BF16)[0:64, 0:128],
                               dqt[:, ib * 64:(ib + 1) * 64], ident_t),
                   reads=[KDQ, KCN], writes=[K(f"PS{b}")], tag=f"dq.d{ec}.tp")
                op("scalar", lambda e, ib=ib, b=b:
                   e.copy(stage[0:64, ib * 128:(ib + 1) * 128],
                          PS[b][:].bitcast(BF16)[0:64, 0:128]),
                   reads=[K(f"PS{b}")], writes=[KSTG], tag=f"dq.d{ec}.ev")
            dma("sync",
                wdc_d.ap().rearrange("(b p) e -> p b e", p=64)[:, :, ec * 128:(ec + 1) * 128],
                stage.rearrange("p (b e) -> p b e", e=128)[0:64, 0:11, :],
                reads=[KSTG], writes=[K("wdc", ec, ec + 1)], tag=f"dq.d{ec}.st")

        if debug:
            dma("gpsimd", dbg["d_wgu"], wguc_d.ap(), reads=[K("wguc")], tag="dbg.wgu")
            dma("gpsimd", dbg["d_wd"], wdc_d.ap(), reads=[K("wdc")], tag="dbg.wd")

        if not dq_ag:
            cur[0] = "skip_ag"
        op("gpsimd", lambda e: e.collective_compute(
            "AllGather", ALU.bypass, replica_groups=[list(range(NC))],
            ins=[wguc_d.ap().opt()], outs=[wgug_d.ap().opt()]),
           reads=[K("wguc")], writes=[K("wgug")], kind="cc", tag="ag.wgu")
        op("gpsimd", lambda e: e.collective_compute(
            "AllGather", ALU.bypass, replica_groups=[list(range(NC))],
            ins=[wdc_d.ap().opt()], outs=[wdg_d.ap().opt()]),
           reads=[K("wdc")], writes=[K("wdg")], kind="cc", tag="ag.wd")
        if not dq_ag:
            cur[0] = "dq"

        cur[0] = "a"
        ln1T = GB1
        KLN1 = lambda fc: K("GB1", fc * 2048, (fc + 1) * 2048)
        xbuf = [GB2[:, 0:4096].bitcast(F32), GB2[:, 4096:8192].bitcast(F32)]
        KX = [K("GB2", 0, 4096), K("GB2", 4096, 8192)]
        ln1bf = GB2[:, 8192:10240]
        KLB = K("GB2", 8192, 10240)
        sqjA = GB2[:, 10240:12288]
        KSQA = K("GB2", 10240, 12288)

        for i in range(NT):
            p = i % 2
            dma("sync", xbuf[p], x_in[i * 128:(i + 1) * 128, :],
                writes=[KX[p]], tag=f"A{i}.x")
            c1, c2c = p, 2 + p
            op("vector", lambda e, p=p, c1=c1:
               e.scalar_tensor_tensor(ln1bf, xbuf[p], 1.0, xbuf[p],
                                      op0=ALU.mult, op1=ALU.bypass,
                                      accum_out=smc(c1)),
               reads=[KX[p]], writes=[KLB, K("SM", c1, c1 + 1)], tag=f"A{i}.s1")
            op("scalar", lambda e, p=p, c2c=c2c:
               e.activation(sqjA, xbuf[p], AF.Square, accum_out=smc(c2c)),
               reads=[KX[p]], writes=[KSQA, K("SM", c2c, c2c + 1)], tag=f"A{i}.s2")
            op("vector", lambda e, c1=c1:
               e.tensor_scalar(smc(4), smc(c1), 1.0 / E, None, op0=ALU.mult),
               strict=[K("SM", c1, c1 + 1)], writes=[K("SM", 4, 5)], tag=f"A{i}.m")
            op("vector", lambda e: e.tensor_tensor(smc(5), smc(4), smc(4), ALU.mult),
               strict=[K("SM", 4, 5)], writes=[K("SM", 5, 6)], tag=f"A{i}.m2")
            op("vector", lambda e, c2c=c2c:
               e.scalar_tensor_tensor(smc(14), smc(c2c), 1.0 / E, smc(5),
                                      op0=ALU.mult, op1=ALU.subtract),
               strict=[K("SM", c2c, c2c + 1), K("SM", 5, 6)],
               writes=[K("SM", 14, 15)], tag=f"A{i}.var")
            rsqrt(smc(6), K("SM", 6, 7), smc(14), K("SM", 14, 15), 1,
                  f"A{i}.rs", eps=LN_EPS)
            op("vector", lambda e:
               e.scalar_tensor_tensor(smc(7), smc(4), -1.0, smc(6),
                                      op0=ALU.mult, op1=ALU.mult),
               strict=[K("SM", 4, 5), K("SM", 6, 7)], writes=[K("SM", 7, 8)],
               tag=f"A{i}.nmr")
            op("scalar", lambda e, p=p:
               e.activation(ln1bf, xbuf[p], AF.Identity, bias=smc(7), scale=smc(6)),
               reads=[KX[p]], strict=[K("SM", 6, 7), K("SM", 7, 8)],
               writes=[KLB], tag=f"A{i}.ap")
            for fc in range(16):
                b = 4 + fc % 2
                op("tensor", lambda e, fc=fc, b=b:
                   e.transpose(PS[b][:].bitcast(BF16)[:, 0:128],
                               ln1bf[:, fc * 128:(fc + 1) * 128], ident_t),
                   reads=[KLB, KCN], writes=[K(f"PS{b}")], tag=f"A{i}.tp{fc}")
                op("vector", lambda e, fc=fc, b=b, i=i:
                   e.tensor_copy(ln1T[:, fc * 2048 + i * 128: fc * 2048 + (i + 1) * 128],
                                 PS[b][:].bitcast(BF16)[:, 0:128]),
                   reads=[K(f"PS{b}")],
                   writes=[K("GB1", fc * 2048 + i * 128, fc * 2048 + (i + 1) * 128)],
                   tag=f"A{i}.ev{fc}")
        if debug:
            dma("gpsimd", dbg["d_ln1T"].rearrange("(c p) t -> p c t", p=128),
                ln1T[:].rearrange("p (c t) -> p c t", t=2048),
                reads=[K("GB1")], tag="dbg.ln1T")

        cur[0] = "b"
        wstg = [GB5[:, 4096:6144].bitcast(F32), GB5[:, 6144:8192].bitcast(F32)]
        KWS = [K("GB5", 4096, 6144), K("GB5", 6144, 8192)]
        wbuf = [(GB2[:, 0:2048], GB2[:, 2048:4096], GB2[:, 4096:6144]),
                (GB2[:, 6144:8192], GB2[:, 8192:10240], GB2[:, 10240:12288])]
        KW = [(K("GB2", 0, 2048), K("GB2", 2048, 4096), K("GB2", 4096, 6144)),
              (K("GB2", 6144, 8192), K("GB2", 8192, 10240), K("GB2", 10240, 12288))]
        qraw = GB2[:, 12288:14336]
        KQRAW = K("GB2", 12288, 14336)
        vfm = GB2[:, 14336:16384]
        KVFM = K("GB2", 14336, 16384)
        qo_t = GB3[:, 0:2048]
        ko_t = GB3[:, 2048:4096]
        kt_t = GB3[:, 4096:6144]
        vt_t = GB3[:, 6144:8192]
        KQO, KKO, KKT, KVT = (K("GB3", 0, 2048), K("GB3", 2048, 4096),
                              K("GB3", 4096, 6144), K("GB3", 6144, 8192))
        kv_t = GB3[:, 8192:12288].bitcast(F32)
        KKV = lambda h: K("GB3", 8192 + h * 256, 8192 + (h + 1) * 256)
        al_t = [GB3[:, 12288:14336], GB3[:, 14336:16384]]
        KAL = [K("GB3", 12288, 14336), K("GB3", 14336, 16384)]
        qtc_t = QTC[:, 0:128]
        stt_t = QTC[:, 128:256]
        kvbf_t = QTC[:, 256:384]
        KQTC, KSTT, KKVB = K("QTC", 0, 128), K("QTC", 128, 256), K("QTC", 256, 384)
        KDM = K("GB4", 8192, 12288)
        KQD = K("GB5", 0, 4096)

        def proj_fm(w_t, kw, out_sb, kout, tagp):
            for tq in range(4):
                for fc in range(16):
                    op("tensor", lambda e, tq=tq, fc=fc, w_t=w_t:
                       e.matmul(PS[tq][:], w_t[:, fc * 128:(fc + 1) * 128],
                                ln1T[:, fc * 2048 + tq * 512: fc * 2048 + (tq + 1) * 512],
                                start=(fc == 0), stop=(fc == 15)),
                       reads=[kw, K("GB1")], writes=[K(f"PS{tq}")],
                       tag=f"{tagp}.mm{tq}.{fc}")
                op("scalar", lambda e, tq=tq, out_sb=out_sb:
                   e.copy(out_sb[:, tq * 512:(tq + 1) * 512], PS[tq][:]),
                   reads=[K(f"PS{tq}")], writes=[kout], tag=f"{tagp}.ev{tq}")

        def rotary(src, ksrc, dst, kdst, tmp, ktmp, tagp):
            op("vector", lambda e: e.tensor_copy(tmp[0:64, :], src[64:128, :]),
               reads=[ksrc], writes=[ktmp], tag=tagp + ".sw0")
            op("vector", lambda e: e.tensor_copy(tmp[64:128, :], src[0:64, :]),
               reads=[ksrc], writes=[ktmp], tag=tagp + ".sw1")
            op("vector", lambda e: e.tensor_tensor(dst, src, c2_t, ALU.mult),
               reads=[ksrc, K("GB4", 0, 4096)], writes=[kdst], tag=tagp + ".c")
            op("vector", lambda e: e.tensor_tensor(tmp, tmp, s2_t, ALU.mult),
               reads=[ktmp, K("GB4", 4096, 8192)], writes=[ktmp], tag=tagp + ".s")
            op("vector", lambda e: e.tensor_tensor(dst, dst, tmp, ALU.add),
               reads=[kdst, ktmp], writes=[kdst], tag=tagp + ".a")

        WSRC = [None, None, None]

        def _wparts(hh):
            p = hh % 2
            parts = []
            for wi, srcw in enumerate([wq_h, wk_h, wv_h]):
                for hv in range(2):
                    parts.append((wi, hv, (2 * wi + hv) % 2, wbuf[p][wi],
                                  KW[p][wi], srcw))
            return parts

        def emit_wload_dma(hh, j):
            wi, hv, sp, wt, kw, srcw = _wparts(hh)[j]
            dma("sync", wstg[sp],
                srcw[:, hh * 2048 + hv * 1024: hh * 2048 + (hv + 1) * 1024],
                writes=[KWS[sp]], tag=f"B{hh}.w{wi}.{hv}")

        def emit_wload_conv(hh, j):
            wi, hv, sp, wt, kw, srcw = _wparts(hh)[j]
            op("scalar", lambda e, wt=wt, hv=hv, sp=sp:
               e.copy(wt[:, hv * 1024:(hv + 1) * 1024], wstg[sp]),
               reads=[KWS[sp]], writes=[kw], tag=f"B{hh}.wc{wi}.{hv}")

        def emit_wload(hh):
            for j in range(6):
                emit_wload_dma(hh, j)
                emit_wload_conv(hh, j)

        PDEST = lambda hh: [(qraw, KQRAW), (al_t[hh % 2], KAL[hh % 2]), (vfm, KVFM)]

        def emit_proj_group(hh, wi, tq):
            p = hh % 2
            w_t, kw = wbuf[p][wi], KW[p][wi]
            dst, kdst = PDEST(hh)[wi]
            for fc in range(16):
                op("tensor", lambda e, tq=tq, fc=fc, w_t=w_t:
                   e.matmul(PS[tq][:], w_t[:, fc * 128:(fc + 1) * 128],
                            ln1T[:, fc * 2048 + tq * 512: fc * 2048 + (tq + 1) * 512],
                            start=(fc == 0), stop=(fc == 15)),
                   reads=[kw, K("GB1")], writes=[K(f"PS{tq}")],
                   tag=f"B{hh}.p{wi}.mm{tq}.{fc}")
            op("scalar", lambda e, tq=tq, dst=dst:
               e.copy(dst[:, tq * 512:(tq + 1) * 512], PS[tq][:]),
               reads=[K(f"PS{tq}")], writes=[kdst], tag=f"B{hh}.p{wi}.ev{tq}")

        def emit_pre(h, feed=lambda n: None):
            p = h % 2
            rotary(qraw, KQRAW, qo_t, KQO, kt_t, KKT, f"B{h}.qr")
            feed(16)
            dma("scalar", qrot_d.ap()[h * 128:(h + 1) * 128, :], qo_t,
                reads=[KQO], writes=[K("qrot", h, h + 1)], tag=f"B{h}.qsp")
            if debug:
                dma("gpsimd", dbg["d_qrot"][h * 128:(h + 1) * 128, :], qo_t,
                    reads=[KQO], tag=f"dbg.qrot{h}")
            rotary(al_t[p], KAL[p], ko_t, KKO, kt_t, KKT, f"B{h}.kr")
            feed(18)
            for c in range(NCH):
                b = 4 + c % 2
                op("tensor", lambda e, c=c, b=b:
                   e.transpose(PS[b][:].bitcast(BF16)[:, 0:128],
                               ko_t[:, c * 128:(c + 1) * 128], ident_t),
                   reads=[KKO, KCN], writes=[K(f"PS{b}")], tag=f"B{h}.ktp{c}")
                op("vector", lambda e, c=c, b=b, h=h:
                   e.tensor_scalar(kt_t[:, c * 128:(c + 1) * 128],
                                   PS[b][:].bitcast(BF16)[:, 0:128],
                                   kd_t[:, h:h + 1], None, op0=ALU.mult),
                   reads=[K(f"PS{b}")], strict=[K("CONST", 128, 160)],
                   writes=[KKT], tag=f"B{h}.kev{c}")
                feed(1)
            for c in range(NCH):
                b = 4 + c % 2
                op("tensor", lambda e, c=c, b=b:
                   e.transpose(PS[b][:].bitcast(BF16)[:, 0:128],
                               vfm[:, c * 128:(c + 1) * 128], ident_t),
                   reads=[KVFM, KCN], writes=[K(f"PS{b}")], tag=f"B{h}.vtp{c}")
                op("vector", lambda e, c=c, b=b:
                   e.tensor_copy(vt_t[:, c * 128:(c + 1) * 128],
                                 PS[b][:].bitcast(BF16)[:, 0:128]),
                   reads=[K(f"PS{b}")], writes=[KVT], tag=f"B{h}.vev{c}")
                feed(1)
            if debug and h == 0:
                dma("gpsimd", dbg["d_ktm"], kt_t, reads=[KKT], tag="dbg.ktm")
                dma("gpsimd", dbg["d_vtm"], vt_t, reads=[KVT], tag="dbg.vtm")
            op("vector", lambda e, h=h:
               e.tensor_tensor(kv_t[:, h * 128:(h + 1) * 128],
                               kv_t[:, h * 128:(h + 1) * 128],
                               kv_t[:, h * 128:(h + 1) * 128], ALU.subtract),
               reads=[KKV(h)], writes=[KKV(h)], tag=f"B{h}.kv0")

        def proj_items(hh):
            items = []
            p = hh % 2
            for g in range(12):
                wi, tq = g // 4, g % 4
                w_t, kw = wbuf[p][wi], KW[p][wi]
                dst, kdst = PDEST(hh)[wi]
                for fc in range(16):
                    items.append(lambda tq=tq, fc=fc, w_t=w_t, kw=kw, hh=hh, wi=wi:
                        op("tensor", lambda e, tq=tq, fc=fc, w_t=w_t:
                           e.matmul(PS[tq][:], w_t[:, fc * 128:(fc + 1) * 128],
                                    ln1T[:, fc * 2048 + tq * 512: fc * 2048 + (tq + 1) * 512],
                                    start=(fc == 0), stop=(fc == 15)),
                           reads=[kw, K("GB1")], writes=[K(f"PS{tq}")],
                           tag=f"B{hh}.p{wi}.mm{tq}.{fc}"))
                items.append(lambda tq=tq, dst=dst, kdst=kdst, hh=hh, wi=wi:
                    op("scalar", lambda e, tq=tq, dst=dst:
                       e.copy(dst[:, tq * 512:(tq + 1) * 512], PS[tq][:]),
                       reads=[K(f"PS{tq}")], writes=[kdst], tag=f"B{hh}.p{wi}.ev{tq}"))
            return items

        def emit_sc(h, c):
            sb_ = 4 + 3 * (c % 2)
            op("tensor", lambda e, c=c, sb_=sb_:
               e.matmul(PS[sb_][:, 0:128], ko_t[:, c * 128:(c + 1) * 128],
                        qo_t[:, c * 128:(c + 1) * 128], start=True, stop=True),
               reads=[KKO, KQO], writes=[K(f"PS{sb_}")], tag=f"B{h}.c{c}.sc")

        def emit_scan_rest(h, c):
            p = h % 2
            cp = c % 2
            rC = float(hc_r_C[h])
            qtc = QTC[:, cp * 128:(cp + 1) * 128]
            stt = QTC[:, 256 + cp * 128:256 + (cp + 1) * 128]
            kvb = QTC[:, 512 + cp * 128:512 + (cp + 1) * 128]
            KQTCc = K("QTC", cp * 128, (cp + 1) * 128)
            KSTTc = K("QTC", 256 + cp * 128, 256 + (cp + 1) * 128)
            KKVBc = K("QTC", 512 + cp * 128, 512 + (cp + 1) * 128)
            sb_ = 4 + 3 * cp
            op("scalar", lambda e, h=h, kvb=kvb:
               e.copy(kvb, kv_t[:, h * 128:(h + 1) * 128]),
               reads=[KKV(h)], writes=[KKVBc], tag=f"B{h}.c{c}.kvbf")
            op("vector", lambda e, c=c, h=h, qtc=qtc:
               e.tensor_tensor(qtc, qo_t[:, c * 128:(c + 1) * 128],
                               qd_t[:, h * 128:(h + 1) * 128], ALU.mult),
               reads=[KQO, KQD], writes=[KQTCc], tag=f"B{h}.c{c}.qtc")
            if c + 1 < NCH:
                emit_sc(h, c + 1)
            op("tensor", lambda e, c=c:
               e.matmul(PS[6][:, 0:128], kt_t[:, c * 128:(c + 1) * 128],
                        vt_t[:, c * 128:(c + 1) * 128], start=True, stop=True),
               reads=[KKT, KVT], writes=[K("PS6")], tag=f"B{h}.c{c}.kp")
            op("vector", lambda e, h=h, sb_=sb_, stt=stt:
               e.tensor_tensor(stt, PS[sb_][:, 0:128],
                               dm_t[:, h * 128:(h + 1) * 128], ALU.mult),
               reads=[K(f"PS{sb_}"), KDM], writes=[KSTTc], tag=f"B{h}.c{c}.st")
            op("tensor", lambda e, c=c, stt=stt:
               e.matmul(PS[5][:, 0:128], vt_t[:, c * 128:(c + 1) * 128], stt,
                        start=True, stop=False),
               reads=[KVT, KSTTc], writes=[K("PS5")], tag=f"B{h}.c{c}.o1")
            op("tensor", lambda e, kvb=kvb, qtc=qtc:
               e.matmul(PS[5][:, 0:128], kvb, qtc, start=False, stop=True),
               reads=[KKVBc, KQTCc], writes=[K("PS5")], tag=f"B{h}.c{c}.o2")
            op("vector", lambda e, h=h, rC=rC:
               e.scalar_tensor_tensor(kv_t[:, h * 128:(h + 1) * 128],
                                      kv_t[:, h * 128:(h + 1) * 128], rC,
                                      PS[6][:, 0:128], op0=ALU.mult, op1=ALU.add),
               reads=[KKV(h), K("PS6")], writes=[KKV(h)], tag=f"B{h}.c{c}.kv")
            op("scalar", lambda e, c=c, p=p:
               e.copy(al_t[p][:, c * 128:(c + 1) * 128], PS[5][:, 0:128]),
               reads=[K("PS5")], writes=[KAL[p]], tag=f"B{h}.c{c}.oev")

        emit_wload(0)
        emit_wload(1)
        for it in proj_items(0):
            it()
        for h in range(H):
            if h + 2 < H:
                for j in range(2):
                    emit_wload_dma(h + 2, j)
            stream = proj_items(h + 1) if h + 1 < H else []
            L = len(stream)
            pos = [0]

            def feed(n, pos=pos, stream=stream, L=L):
                e = min(pos[0] + n, L)
                while pos[0] < e:
                    stream[pos[0]]()
                    pos[0] += 1

            emit_pre(h, feed)
            base = pos[0]
            prev_t = base
            emit_sc(h, 0)
            for c in range(NCH):
                if h + 2 < H and c % 2 == 1 and c < 12:
                    emit_wload_conv(h + 2, c // 2)
                    if c // 2 + 2 < 6:
                        emit_wload_dma(h + 2, c // 2 + 2)
                tgt = base + (L - base) * (c + 1) // NCH
                mid = (prev_t + tgt + 1) // 2
                feed(mid - pos[0])
                emit_scan_rest(h, c)
                feed(tgt - pos[0])
                prev_t = tgt
            dma("scalar", attnl_d.ap()[h * 128:(h + 1) * 128, :], al_t[h % 2],
                reads=[KAL[h % 2]], writes=[K("attnl", h, h + 1)], tag=f"B{h}.alsp")
            if h % 4 == 3:
                g = h // 4
                dma("gpsimd", ag2i_g[g].ap().rearrange("(hh p) d -> p hh d", p=128),
                    kv_t[:, g * 512:(g + 1) * 512].rearrange(
                        "p (hh d) -> p hh d", d=128),
                    reads=[K("GB3", 8192 + g * 1024, 8192 + (g + 1) * 1024)],
                    writes=[K(f"ag2i{g}")], tag=f"ag2.in{g}")
                op("gpsimd", lambda e, g=g: e.collective_compute(
                    "AllGather", ALU.bypass, replica_groups=[list(range(NC))],
                    ins=[ag2i_g[g].ap().opt()], outs=[ag2o_g[g].ap().opt()]),
                   reads=[K(f"ag2i{g}")], writes=[K(f"ag2o{g}")], kind="cc",
                   tag=f"ag2.cc{g}")

        cur[0] = "ag2"
        if debug:
            for g in range(4):
                dma("gpsimd", dbg["d_kv"][g * 512:(g + 1) * 512, :],
                    ag2i_g[g].ap(), reads=[K(f"ag2i{g}")], tag=f"dbg.kv{g}")
        dma("gpsimd", agb_i.ap(), ag2i_g[3].ap()[0:128, :],
            reads=[K("ag2i3")], writes=[K("agbi")], tag="agb.cp")
        op("gpsimd", lambda e: e.collective_compute(
            "AllGather", ALU.bypass, replica_groups=[list(range(NC))],
            ins=[agb_i.ap().opt()], outs=[agb_o.ap().opt()]),
           reads=[K("agbi")],
           writes=[K("ag2o3"), K("wgug"), K("wdg")], kind="cc", tag="agb.cc")

        gath2 = [GB2[:, 0:2048].bitcast(F32), GB2[:, 6144:8192].bitcast(F32)]
        KGATH2 = [K("GB2", 0, 2048), K("GB2", 6144, 8192)]
        inacc = GB2[:, 2048:2304].bitcast(F32)
        KINA = K("GB2", 2048, 2304)
        inacc2 = GB2[:, 2432:2688].bitcast(F32)
        KINA2 = K("GB2", 2432, 2688)
        inbf = GB2[:, 2304:2432]
        KINB = K("GB2", 2304, 2432)
        sqc = GB2[:, 4096:6144]
        KSQC = K("GB2", 4096, 6144)
        qprep = GB2[:, 12288:14336]
        KQP = K("GB2", 12288, 14336)
        qo_s4 = [GB3[:, 0:2048], GB3[:, 2048:4096],
                 GB3[:, 8192:10240], GB3[:, 10240:12288]]
        KQOS4 = [K("GB3", 0, 2048), K("GB3", 2048, 4096),
                 K("GB3", 8192, 10240), K("GB3", 10240, 12288)]
        co_t4 = [GB3[:, 4096:6144], GB3[:, 6144:8192],
                 GB3[:, 12288:14336], GB3[:, 14336:16384]]
        KCO4 = [K("GB3", 4096, 6144), K("GB3", 6144, 8192),
                K("GB3", 12288, 14336), K("GB3", 14336, 16384)]

        def c_gath(h):
            g, hh, p2 = h // 4, h % 4, h % 2
            deps = [K(f"ag2o{g}")] + ([K(f"ag2o{g + 1}")] if g < 3 else [])
            for rh in range(2):
                dma("sync",
                    gath2[p2].rearrange("p (r d) -> p r d", d=128)
                    [:, rh * 4:(rh + 1) * 4, :],
                    ag2o_g[g].ap().rearrange("(r hh p) d -> p r hh d", r=NC, p=128)
                    [:, rh * 4:(rh + 1) * 4, hh, :],
                    reads=deps, writes=[KGATH2[p2]], tag=f"C{h}.gth{rh}")

        def c_qld(h):
            s = h % 4
            for hv in range(2):
                dma("sync", qo_s4[s][:, hv * 1024:(hv + 1) * 1024],
                    qrot_d.ap()[h * 128:(h + 1) * 128, hv * 1024:(hv + 1) * 1024],
                    reads=[K("qrot", h, h + 1)], writes=[KQOS4[s]],
                    tag=f"C{h}.qld{hv}")

        def c_ald(h):
            s = h % 4
            for hv in range(2):
                dma("sync", co_t4[s][:, hv * 1024:(hv + 1) * 1024],
                    attnl_d.ap()[h * 128:(h + 1) * 128, hv * 1024:(hv + 1) * 1024],
                    reads=[K("attnl", h, h + 1)], writes=[KCO4[s]],
                    tag=f"C{h}.ald{hv}")

        c_gath(0)
        for h0 in range(2):
            c_qld(h0)
            c_ald(h0)

        for h in range(H):
            p = h % 2
            s4 = h % 4
            if h + 1 < H:
                c_gath(h + 1)
            if h + 2 < H:
                c_qld(h + 2)
                c_ald(h + 2)
            for cg in (2 * h, 2 * h + 1):
                sp = cg % 2
                dma("gpsimd", wstg[sp], wo_h[:, cg * 1024:(cg + 1) * 1024],
                    writes=[KWS[sp]], tag=f"W.ld{cg}")
                op("scalar", lambda e, cg=cg, sp=sp:
                   e.copy(GB1[:, cg * 1024:(cg + 1) * 1024], wstg[sp]),
                   reads=[KWS[sp]], writes=[K("GB1", cg * 1024, (cg + 1) * 1024)],
                   tag=f"W.cv{cg}")
            op("vector", lambda e, ein=h:
               e.scalar_tensor_tensor(GB1[:, ein * 2048:(ein + 1) * 2048],
                                      GB1[:, ein * 2048:(ein + 1) * 2048],
                                      rmsw_t[:, ein:ein + 1],
                                      GB1[:, ein * 2048:(ein + 1) * 2048],
                                      op0=ALU.mult, op1=ALU.bypass),
               reads=[K("GB1", h * 2048, (h + 1) * 2048)],
               strict=[K("CONST", 160, 192)],
               writes=[K("GB1", h * 2048, (h + 1) * 2048)], tag=f"W.rw{h}")
            op("vector", lambda e:
               e.tensor_tensor(inacc, inacc, inacc, ALU.subtract),
               reads=[KINA], writes=[KINA], tag=f"C{h}.inz")
            accs = [inacc, inacc2]
            kaccs = [KINA, KINA2]
            for r_ in range(0, NC):
                src_a, dst_a = accs[r_ % 2], accs[1 - r_ % 2]
                ksrc, kdst = kaccs[r_ % 2], kaccs[1 - r_ % 2]
                op("vector", lambda e, r_=r_, h=h, s_=src_a, d_=dst_a, p=p:
                   e.scalar_tensor_tensor(d_, gath2[p][:, r_ * 128:(r_ + 1) * 128],
                                          wcomb_t[:, h * 8 + r_:h * 8 + r_ + 1],
                                          s_, op0=ALU.mult, op1=ALU.add),
                   reads=[KGATH2[p], ksrc], strict=[K("CONST", 192, 448)],
                   writes=[kdst], tag=f"C{h}.in{r_}")
            op("scalar", lambda e: e.copy(inbf, inacc2),
               reads=[KINA2], writes=[KINB], tag=f"C{h}.inb")
            if debug:
                dma("gpsimd", dbg["d_in"][h * 128:(h + 1) * 128, :], inacc,
                    reads=[KINA], tag=f"dbg.in{h}")
            rC = float(hc_r_C[h])
            for l_ in range(NCH):
                op("vector", lambda e, l_=l_, s4=s4, h=h, rl=float(rC ** l_):
                   e.scalar_tensor_tensor(qprep[:, l_ * 128:(l_ + 1) * 128],
                                          qo_s4[s4][:, l_ * 128:(l_ + 1) * 128], rl,
                                          qd_t[:, h * 128:(h + 1) * 128],
                                          op0=ALU.mult, op1=ALU.mult),
                   reads=[KQOS4[s4], KQD], writes=[KQP], tag=f"C{h}.qp{l_}")
            for tq in range(4):
                op("tensor", lambda e, tq=tq:
                   e.matmul(PS[tq][:], inbf, qprep[:, tq * 512:(tq + 1) * 512],
                            start=True, stop=True),
                   reads=[KINB, KQP], writes=[K(f"PS{tq}")], tag=f"C{h}.dmm{tq}")
            for tq in range(4):
                op("vector", lambda e, tq=tq, s4=s4:
                   e.tensor_tensor(co_t4[s4][:, tq * 512:(tq + 1) * 512],
                                   co_t4[s4][:, tq * 512:(tq + 1) * 512],
                                   PS[tq][:], ALU.add),
                   reads=[KCO4[s4], K(f"PS{tq}")], writes=[KCO4[s4]],
                   tag=f"C{h}.add{tq}")
            dma("scalar", attnc_d.ap()[h * 128:(h + 1) * 128, :], co_t4[s4],
                reads=[KCO4[s4]], writes=[K("attnc", h, h + 1)], tag=f"C{h}.csp")
            op("scalar", lambda e, s4=s4: e.activation(sqc, co_t4[s4], AF.Square),
               reads=[KCO4[s4]], writes=[KSQC], tag=f"C{h}.sq")
            for tq in range(4):
                op("tensor", lambda e, tq=tq, h=h:
                   e.matmul(PS[4 + tq][0:1, :], ones_col,
                            sqc[:, tq * 512:(tq + 1) * 512],
                            start=(h == 0), stop=(h == H - 1)),
                   reads=[KSQC, KCN], writes=[K(f"PS{4 + tq}")], tag=f"C{h}.ssq{tq}")
        if debug:
            dma("gpsimd", dbg["d_attnl"], attnl_d.ap(), reads=[K("attnl")], tag="dbg.al")
            dma("gpsimd", dbg["d_attnc"], attnc_d.ap(), reads=[K("attnc")], tag="dbg.ac")

        for tq in range(4):
            op("vector", lambda e, tq=tq:
               e.tensor_scalar(ROW[0:1, tq * 512:(tq + 1) * 512], PS[4 + tq][0:1, :],
                               1.0 / E, RMS_EPS, op0=ALU.mult, op1=ALU.add),
               reads=[K(f"PS{4 + tq}")], writes=[K("ROW", tq * 512, (tq + 1) * 512)],
               tag=f"R.m{tq}")
        op("scalar", lambda e: e.copy(ROWB[0:1, :], ROW[0:1, 0:2048]),
           reads=[K("ROW", 0, 2048)], writes=[K("ROWB")], tag="R.bf")
        for i in range(NT):
            op("tensor", lambda e, i=i:
               e.matmul(PS[4][:, i:i + 1], ROWB[0:1, i * 128:(i + 1) * 128],
                        ones_row[0:1, 0:1], start=(i == 0), stop=(i == NT - 1)),
               reads=[K("ROWB"), KCN], writes=[K("PS4")], tag=f"R.c{i}")
        op("vector", lambda e: e.tensor_copy(SM[:, 48:64], PS[4][:, 0:16]),
           reads=[K("PS4")], writes=[K("SM", 48, 64)], tag="R.ev")
        for i in range(NT):
            rsqrt(SM[:, 16 + i:17 + i], K("SM", 16 + i, 17 + i),
                  SM[:, 48 + i:49 + i], K("SM", 48 + i, 49 + i), 1, f"R.rs{i}")

        cur[0] = "c"
        ant_t = [GB3[:, 4096:6144], GB3[:, 6144:8192]]
        KANT = [K("GB3", 4096, 6144), K("GB3", 6144, 8192)]
        xc_t2 = [GB3[:, 8192:12288].bitcast(F32), GB3[:, 0:4096].bitcast(F32)]
        KXC2 = [K("GB3", 8192, 12288), K("GB3", 0, 4096)]
        xmid_t = [GB5[:, 0:4096].bitcast(F32), GB5[:, 4096:8192].bitcast(F32)]
        KXM = [K("GB5", 0, 4096), K("GB5", 4096, 8192)]
        yc_t2 = [GB4[:, 8192:12288].bitcast(F32), GB2[:, 4096:8192].bitcast(F32)]
        KYC2 = [K("GB4", 8192, 12288), K("GB2", 4096, 8192)]
        lnv_t = GB4[:, 0:4096].bitcast(F32)
        KLNV = K("GB4", 0, 4096)
        sqc2 = GB4[:, 4096:6144]
        KSQ2 = K("GB4", 4096, 6144)
        ln2b2 = [GB4[:, 6144:8192], GB2[:, 0:2048]]
        KL2B2 = [K("GB4", 6144, 8192), K("GB2", 0, 2048)]
        l2s_t = [GB3[:, 12288:14336], GB3[:, 14336:16384]]
        KL2S = [K("GB3", 12288, 14336), K("GB3", 14336, 16384)]

        def emit_tp4(i, g):
            p = i % 2
            for fc in range(g * 4, g * 4 + 4):
                b = 4 + fc % 2
                op("tensor", lambda e, fc=fc, b=b, p=p:
                   e.transpose(PS[b][:].bitcast(BF16)[:, 0:128],
                               ln2b2[p][:, fc * 128:(fc + 1) * 128], ident_t),
                   reads=[KL2B2[p], KCN], writes=[K(f"PS{b}")], tag=f"D{i}.tp{fc}")
                op("vector", lambda e, fc=fc, b=b, p=p:
                   e.tensor_copy(l2s_t[p][:, fc * 128:(fc + 1) * 128],
                                 PS[b][:].bitcast(BF16)[:, 0:128]),
                   reads=[K(f"PS{b}")], writes=[KL2S[p]], tag=f"D{i}.tev{fc}")
            if g == 3:
                dma("scalar",
                    ln2T_d.ap().rearrange("(c p) t -> p c t", p=128)[:, :, i * 128:(i + 1) * 128],
                    l2s_t[p].rearrange("p (c t) -> p c t", t=128),
                    reads=[KL2S[p]], writes=[K("ln2T", i, i + 1)], tag=f"D{i}.l2sp")

        for i in range(NT):
            p = i % 2
            yc_t, KYC = yc_t2[p], KYC2[p]
            ln2b, KL2B = ln2b2[p], KL2B2[p]
            xc_t, KXC = xc_t2[p], KXC2[p]
            dma("sync", ant_t[p].rearrange("p (c t) -> p c t", t=128),
                attnc_d.ap().rearrange("(c q) t -> q c t", q=128)[:, :, i * 128:(i + 1) * 128],
                reads=[K("attnc")], writes=[KANT[p]], tag=f"D{i}.ant")
            for hv in range(2):
                dma("sync", xc_t[:, hv * 1024:(hv + 1) * 1024],
                    x_in[i * 128:(i + 1) * 128, hv * 1024:(hv + 1) * 1024],
                    writes=[KXC], tag=f"D{i}.x{hv}")
            for tq in range(4):
                for ein in range(16):
                    op("tensor", lambda e, tq=tq, ein=ein, p=p:
                       e.matmul(PS[tq][:], ant_t[p][:, ein * 128:(ein + 1) * 128],
                                GB1[:, ein * 2048 + tq * 512: ein * 2048 + (tq + 1) * 512],
                                start=(ein == 0), stop=(ein == 15)),
                       reads=[KANT[p], K("GB1", ein * 2048, (ein + 1) * 2048)],
                       writes=[K(f"PS{tq}")], tag=f"D{i}.mm{tq}.{ein}")
                if i > 0:
                    emit_tp4(i - 1, tq)
            for tq in range(4):
                op("vector", lambda e, tq=tq, i=i, yc_t=yc_t:
                   e.tensor_scalar(yc_t[:, tq * 512:(tq + 1) * 512], PS[tq][:],
                                   SM[:, 16 + i:17 + i], 0.0, op0=ALU.mult,
                                   op1=ALU.add, accum_out=smc(tq)),
                   reads=[K(f"PS{tq}")], strict=[K("SM", 16 + i, 17 + i)],
                   writes=[KYC, K("SM", tq, tq + 1)], tag=f"D{i}.ev{tq}")
            op("vector", lambda e: e.tensor_tensor(smc(0), smc(0), smc(1), ALU.add),
               strict=[K("SM", 0, 1), K("SM", 1, 2)], writes=[K("SM", 0, 1)], tag=f"D{i}.cmb1")
            op("vector", lambda e: e.tensor_tensor(smc(2), smc(2), smc(3), ALU.add),
               strict=[K("SM", 2, 3), K("SM", 3, 4)], writes=[K("SM", 2, 3)], tag=f"D{i}.cmb2")
            op("vector", lambda e: e.tensor_tensor(smc(0), smc(0), smc(2), ALU.add),
               strict=[K("SM", 0, 1), K("SM", 2, 3)], writes=[K("SM", 0, 1)], tag=f"D{i}.cmb3")
            op("scalar", lambda e, yc_t=yc_t:
               e.activation(sqc2, yc_t, AF.Square, accum_out=smc(8)),
               reads=[KYC], writes=[KSQ2, K("SM", 8, 9)], tag=f"D{i}.sq")
            op("vector", lambda e: e.tensor_scalar(smc(4), smc(0), 1.0 / E, None,
                                                   op0=ALU.mult),
               strict=[K("SM", 0, 1)], writes=[K("SM", 4, 5)], tag=f"D{i}.m")
            op("vector", lambda e: e.tensor_tensor(smc(5), smc(4), smc(4), ALU.mult),
               strict=[K("SM", 4, 5)], writes=[K("SM", 5, 6)], tag=f"D{i}.m2")
            op("vector", lambda e:
               e.scalar_tensor_tensor(smc(14), smc(8), 1.0 / E, smc(5),
                                      op0=ALU.mult, op1=ALU.subtract),
               strict=[K("SM", 8, 9), K("SM", 5, 6)], writes=[K("SM", 14, 15)],
               tag=f"D{i}.var")
            rsqrt(smc(6), K("SM", 6, 7), smc(14), K("SM", 14, 15), 1,
                  f"D{i}.rs", eps=LN_EPS)
            op("vector", lambda e:
               e.scalar_tensor_tensor(smc(7), smc(4), -1.0, smc(6),
                                      op0=ALU.mult, op1=ALU.mult),
               strict=[K("SM", 4, 5), K("SM", 6, 7)], writes=[K("SM", 7, 8)],
               tag=f"D{i}.nm")
            op("scalar", lambda e, yc_t=yc_t:
               e.activation(lnv_t, yc_t, AF.Identity, bias=smc(7), scale=smc(6)),
               reads=[KYC], strict=[K("SM", 6, 7), K("SM", 7, 8)], writes=[KLNV],
               tag=f"D{i}.ap")
            op("vector", lambda e, p=p, xc_t=xc_t:
               e.scalar_tensor_tensor(xmid_t[p], lnv_t, 1.0, xc_t,
                                      op0=ALU.mult, op1=ALU.add, accum_out=smc(9)),
               reads=[KLNV, KXC], writes=[KXM[p], K("SM", 9, 10)], tag=f"D{i}.xm")
            dma("scalar", xmid_d.ap()[i * 128:(i + 1) * 128, :], xmid_t[p],
                reads=[KXM[p]], writes=[K("xmid", i, i + 1)], tag=f"D{i}.xsp")
            op("scalar", lambda e, p=p:
               e.activation(sqc2, xmid_t[p], AF.Square, accum_out=smc(10)),
               reads=[KXM[p]], writes=[KSQ2, K("SM", 10, 11)], tag=f"D{i}.sq2")
            op("vector", lambda e: e.tensor_scalar(smc(4), smc(9), 1.0 / E, None,
                                                   op0=ALU.mult),
               strict=[K("SM", 9, 10)], writes=[K("SM", 4, 5)], tag=f"D{i}.m_2")
            op("vector", lambda e: e.tensor_tensor(smc(5), smc(4), smc(4), ALU.mult),
               strict=[K("SM", 4, 5)], writes=[K("SM", 5, 6)], tag=f"D{i}.m2_2")
            op("vector", lambda e:
               e.scalar_tensor_tensor(smc(14), smc(10), 1.0 / E, smc(5),
                                      op0=ALU.mult, op1=ALU.subtract),
               strict=[K("SM", 10, 11), K("SM", 5, 6)], writes=[K("SM", 14, 15)],
               tag=f"D{i}.var2")
            rsqrt(smc(6), K("SM", 6, 7), smc(14), K("SM", 14, 15), 1,
                  f"D{i}.rs2", eps=LN_EPS)
            op("vector", lambda e:
               e.scalar_tensor_tensor(smc(7), smc(4), -1.0, smc(6),
                                      op0=ALU.mult, op1=ALU.mult),
               strict=[K("SM", 4, 5), K("SM", 6, 7)], writes=[K("SM", 7, 8)],
               tag=f"D{i}.nm2")
            op("scalar", lambda e, p=p, ln2b=ln2b:
               e.activation(ln2b, xmid_t[p], AF.Identity, bias=smc(7), scale=smc(6)),
               reads=[KXM[p]], strict=[K("SM", 6, 7), K("SM", 7, 8)], writes=[KL2B],
               tag=f"D{i}.ap2")
        for g in range(4):
            emit_tp4(NT - 1, g)
        if debug:
            dma("gpsimd", dbg["d_xmid"], xmid_d.ap(), reads=[K("xmid")], tag="dbg.xm")
            dma("gpsimd", dbg["d_ln2T"], ln2T_d.ap(), reads=[K("ln2T")], tag="dbg.l2")

        cur[0] = "e"
        l2h = GB2
        KL2H = K("GB2")
        wgb = [GB3[:, 0:2048], GB3[:, 2048:4096]]
        KWG = [K("GB3", 0, 2048), K("GB3", 2048, 4096)]
        wub = [GB3[:, 4096:6144], GB3[:, 6144:8192]]
        KWU = [K("GB3", 4096, 6144), K("GB3", 6144, 8192)]
        wdb = [GB3[:, 8192:10240], GB3[:, 10240:12288]]
        KWD = [K("GB3", 8192, 10240), K("GB3", 10240, 12288)]
        silg = GB3[:, 12288:14336].bitcast(F32)
        KSIL = K("GB3", 12288, 14336)
        yn_t = GB5[:, 0:4096].bitcast(F32)
        KYN = K("GB5", 0, 4096)
        xmi_t = GB5[:, 4096:8192].bitcast(F32)
        KXMI = K("GB5", 4096, 8192)

        def hK(j):
            if j < 32:
                return K("GB1", j * 1024, (j + 1) * 1024)
            return K("GB4", (j - 32) * 1024, (j - 31) * 1024)

        def h_ap(j):
            if j < 32:
                return GB1[:, j * 1024:(j + 1) * 1024]
            return GB4[:, (j - 32) * 1024:(j - 31) * 1024]

        for hf in range(2):
            for fc in range(16):
                dma("sync", l2h[:, fc * 1024:(fc + 1) * 1024],
                    ln2T_d.ap()[fc * 128:(fc + 1) * 128, hf * 1024:(hf + 1) * 1024],
                    reads=[K("ln2T")], writes=[K("GB2", fc * 1024, (fc + 1) * 1024)],
                    tag=f"E{hf}.l2{fc}")
            for j in range(NJ):
                p = j % 2
                rg_, jj = j // 11, j % 11
                ru_ = rg_ + 4
                for hv in range(2):
                    dma("sync", wgb[p][:, hv * 1024:(hv + 1) * 1024],
                        wgug_d.ap()[(rg_ * 11 + jj) * 128:(rg_ * 11 + jj + 1) * 128,
                                    hv * 1024:(hv + 1) * 1024],
                        writes=[KWG[p]], tag=f"E{hf}.g{j}.wg{hv}")
                    dma("sync", wub[p][:, hv * 1024:(hv + 1) * 1024],
                        wgug_d.ap()[(ru_ * 11 + jj) * 128:(ru_ * 11 + jj + 1) * 128,
                                    hv * 1024:(hv + 1) * 1024],
                        writes=[KWU[p]], tag=f"E{hf}.g{j}.wu{hv}")
                for tq in range(2):
                    for fc in range(16):
                        op("tensor", lambda e, tq=tq, fc=fc, p=p:
                           e.matmul(PS[tq][:], wgb[p][:, fc * 128:(fc + 1) * 128],
                                    l2h[:, fc * 1024 + tq * 512: fc * 1024 + (tq + 1) * 512],
                                    start=(fc == 0), stop=(fc == 15)),
                           reads=[KWG[p], KL2H], writes=[K(f"PS{tq}")],
                           tag=f"E{hf}.g{j}.gm{tq}")
                    for fc in range(16):
                        op("tensor", lambda e, tq=tq, fc=fc, p=p:
                           e.matmul(PS[2 + tq][:], wub[p][:, fc * 128:(fc + 1) * 128],
                                    l2h[:, fc * 1024 + tq * 512: fc * 1024 + (tq + 1) * 512],
                                    start=(fc == 0), stop=(fc == 15)),
                           reads=[KWU[p], KL2H], writes=[K(f"PS{2 + tq}")],
                           tag=f"E{hf}.g{j}.um{tq}")
                for tq in range(2):
                    op("scalar", lambda e, tq=tq:
                       e.activation(silg[:, tq * 512:(tq + 1) * 512], PS[tq][:], AF.Silu),
                       reads=[K(f"PS{tq}")], writes=[KSIL], tag=f"E{hf}.g{j}.si{tq}")
                    op("vector", lambda e, tq=tq, j=j:
                       e.tensor_tensor(h_ap(j)[:, tq * 512:(tq + 1) * 512],
                                       PS[2 + tq][:], silg[:, tq * 512:(tq + 1) * 512],
                                       ALU.mult),
                       reads=[K(f"PS{2 + tq}"), KSIL], writes=[hK(j)],
                       tag=f"E{hf}.g{j}.h{tq}")
            SL = [8, 9, 10, 11, 0, 1, 2, 3, 4, 5, 6, 7]
            wdsl = lambda s: GB3[:, SL[s] * 1024:(SL[s] + 1) * 1024]
            KWDS = lambda s: K("GB3", SL[s] * 1024, (SL[s] + 1) * 1024)
            ynb = lambda ti: GB2[:, ti * 4096:(ti + 1) * 4096].bitcast(F32)
            KYNB = lambda ti: K("GB2", ti * 4096, (ti + 1) * 4096)
            sched = [(pr2, eh, j) for pr2 in range(2) for eh in range(2)
                     for j in range(NJ)]

            def wd_load(idx, hf=hf):
                pr2, eh, j = sched[idx]
                s = idx % 12
                dma("gpsimd", wdsl(s),
                    wdg_d.ap()[j * 128:(j + 1) * 128, eh * 1024:(eh + 1) * 1024],
                    writes=[KWDS(s)], tag=f"E{hf}.wd{idx}")

            DEPTH = 6
            for idx in range(DEPTH):
                wd_load(idx)
            for idx, (pr2, eh, j) in enumerate(sched):
                s = idx % 12
                for ti in range(4):
                    t_ = pr2 * 4 + ti
                    for q2 in range(2):
                        op("tensor", lambda e, ti=ti, q2=q2, t_=t_, j=j, s=s:
                           e.matmul(PS[ti * 2 + q2][:],
                                    h_ap(j)[:, t_ * 128:(t_ + 1) * 128],
                                    wdsl(s)[:, q2 * 512:(q2 + 1) * 512],
                                    start=(j == 0), stop=(j == NJ - 1)),
                           reads=[hK(j), KWDS(s)], writes=[K(f"PS{ti * 2 + q2}")],
                           tag=f"E{hf}.d{idx}.{ti}{q2}")
                if idx + DEPTH < len(sched):
                    wd_load(idx + DEPTH)
                if j != NJ - 1:
                    continue
                for ti in range(4):
                    for q2 in range(2):
                        ac = 64 + ti * 4 + eh * 2 + q2
                        op("vector", lambda e, ti=ti, q2=q2, eh=eh, ac=ac:
                           e.tensor_scalar(
                               ynb(ti)[:, eh * 1024 + q2 * 512:
                                       eh * 1024 + (q2 + 1) * 512],
                               PS[ti * 2 + q2][:], 1.0, 0.0,
                               op0=ALU.mult, op1=ALU.add, accum_out=smc(ac)),
                           reads=[K(f"PS{ti * 2 + q2}")],
                           writes=[KYNB(ti), K("SM", ac, ac + 1)],
                           tag=f"E{hf}.p{pr2}e{eh}.yc{ti}{q2}")
                if eh != 1:
                    continue
                for ti in range(4):
                    t_ = pr2 * 4 + ti
                    i_gl = hf * 8 + t_
                    a0 = 64 + ti * 4
                    op("vector", lambda e, a0=a0:
                       e.tensor_tensor(smc(0), smc(a0), smc(a0 + 1), ALU.add),
                       strict=[K("SM", a0, a0 + 1), K("SM", a0 + 1, a0 + 2)],
                       writes=[K("SM", 0, 1)], tag=f"E{hf}.v{t_}.c1")
                    op("vector", lambda e, a0=a0:
                       e.tensor_tensor(smc(2), smc(a0 + 2), smc(a0 + 3), ALU.add),
                       strict=[K("SM", a0 + 2, a0 + 3), K("SM", a0 + 3, a0 + 4)],
                       writes=[K("SM", 2, 3)], tag=f"E{hf}.v{t_}.c2")
                    op("vector", lambda e: e.tensor_tensor(smc(0), smc(0), smc(2), ALU.add),
                       strict=[K("SM", 0, 1), K("SM", 2, 3)], writes=[K("SM", 0, 1)],
                       tag=f"E{hf}.v{t_}.c3")
                    op("scalar", lambda e, ti=ti:
                       e.activation(GB3[:, 12288:14336], ynb(ti), AF.Square,
                                    accum_out=smc(8)),
                       reads=[KYNB(ti)], writes=[KSIL, K("SM", 8, 9)],
                       tag=f"E{hf}.v{t_}.sq")
                    op("vector", lambda e: e.tensor_scalar(smc(4), smc(0), 1.0 / E, None,
                                                           op0=ALU.mult),
                       strict=[K("SM", 0, 1)], writes=[K("SM", 4, 5)],
                       tag=f"E{hf}.v{t_}.m")
                    op("vector", lambda e:
                       e.tensor_scalar(smc(15), smc(8), 1.0 / E, FFN_EPS,
                                       op0=ALU.mult, op1=ALU.add),
                       strict=[K("SM", 8, 9)], writes=[K("SM", 15, 16)],
                       tag=f"E{hf}.v{t_}.me")
                    rsqrt(smc(6), K("SM", 6, 7), smc(15), K("SM", 15, 16), 1,
                          f"E{hf}.v{t_}.rs")
                    op("vector", lambda e: e.tensor_tensor(smc(5), smc(4), smc(4), ALU.mult),
                       strict=[K("SM", 4, 5)], writes=[K("SM", 5, 6)],
                       tag=f"E{hf}.v{t_}.m2")
                    op("vector", lambda e:
                       e.scalar_tensor_tensor(smc(14), smc(8), 1.0 / E, smc(5),
                                              op0=ALU.mult, op1=ALU.subtract),
                       strict=[K("SM", 8, 9), K("SM", 5, 6)], writes=[K("SM", 14, 15)],
                       tag=f"E{hf}.v{t_}.var")
                    op("vector", lambda e: e.tensor_tensor(smc(11), smc(6), smc(6), ALU.mult),
                       strict=[K("SM", 6, 7)], writes=[K("SM", 11, 12)],
                       tag=f"E{hf}.v{t_}.ir2")
                    op("vector", lambda e: e.tensor_tensor(smc(12), smc(14), smc(11), ALU.mult),
                       strict=[K("SM", 14, 15), K("SM", 11, 12)], writes=[K("SM", 12, 13)],
                       tag=f"E{hf}.v{t_}.vyn")
                    rsqrt(smc(7), K("SM", 7, 8), smc(12), K("SM", 12, 13), 1,
                          f"E{hf}.v{t_}.rs3", eps=LN_EPS)
                    op("vector", lambda e: e.tensor_tensor(smc(9), smc(6), smc(7), ALU.mult),
                       strict=[K("SM", 6, 7), K("SM", 7, 8)], writes=[K("SM", 9, 10)],
                       tag=f"E{hf}.v{t_}.sc")
                    op("vector", lambda e:
                       e.scalar_tensor_tensor(smc(10), smc(4), -1.0, smc(9),
                                              op0=ALU.mult, op1=ALU.mult),
                       strict=[K("SM", 4, 5), K("SM", 9, 10)], writes=[K("SM", 10, 11)],
                       tag=f"E{hf}.v{t_}.nm")
                    dma("sync", xmi_t, xmid_d.ap()[i_gl * 128:(i_gl + 1) * 128, :],
                        reads=[K("xmid", i_gl, i_gl + 1)], writes=[KXMI],
                        tag=f"E{hf}.v{t_}.xl")
                    op("scalar", lambda e, ti=ti:
                       e.activation(ynb(ti), ynb(ti), AF.Identity,
                                    bias=smc(10), scale=smc(9)),
                       reads=[KYNB(ti)], strict=[K("SM", 9, 10), K("SM", 10, 11)],
                       writes=[KYNB(ti)], tag=f"E{hf}.v{t_}.ap")
                    op("vector", lambda e, ti=ti:
                       e.tensor_tensor(xmi_t, xmi_t, ynb(ti), ALU.add),
                       reads=[KXMI, KYNB(ti)], writes=[KXMI], tag=f"E{hf}.v{t_}.add")
                    dma("sync", out_h[i_gl * 128:(i_gl + 1) * 128, :], xmi_t,
                        reads=[KXMI], writes=[K("outd", i_gl, i_gl + 1)],
                        tag=f"E{hf}.v{t_}.out")
                    if postdump:
                        for k_, col in enumerate((13, 9, 12, 14, 6, 7, 4, 8)):
                            op("vector", lambda e, k_=k_, col=col, i_gl=i_gl:
                               e.tensor_tensor(STATD[:, i_gl * 8 + k_:i_gl * 8 + k_ + 1],
                                               smc(col), smc(col), ALU.max),
                               strict=[K("SM", col, col + 1)],
                               writes=[K("STATD", i_gl * 8 + k_, i_gl * 8 + k_ + 1)],
                               tag=f"E{hf}.v{t_}.st{k_}")


        op("scalar", lambda e: e.copy(SM[0:1, 400:401], SM[0:1, 400:401]),
           reads=[K("outd", 0, NT)], tag="fence")
        if postdump:
            dma("gpsimd", dbg["p_stat"], STATD[:],
                reads=[K("STATD")], tag="pd.stat")
            dma("gpsimd", dbg["p_xmid"], xmid_d.ap(),
                reads=[K("outd", 0, NT), K("xmid")], tag="pd.xmid")
            dma("gpsimd", dbg["p_ln2T"], ln2T_d.ap(),
                reads=[K("outd", 0, NT), K("ln2T")], tag="pd.ln2T")
            dma("gpsimd", dbg["p_attnc"], attnc_d.ap(),
                reads=[K("outd", 0, NT), K("attnc")], tag="pd.attnc")
            dma("gpsimd", dbg["p_attnl"], attnl_d.ap(),
                reads=[K("outd", 0, NT), K("attnl")], tag="pd.attnl")

        P.emit(block, sems)
    return nc, dbg


def _mk_inmap(core, inputs):
    hc = host_consts(core)
    b, s = core // GRP, core % GRP
    x = np.ascontiguousarray(inputs["L0_attn_x"][b, s * TOK:(s + 1) * TOK, :])
    gus_t = np.ascontiguousarray(
        np.asarray(inputs["L0_ffn_gate_up_scales"]).T
        [core * GU_SH:(core + 1) * GU_SH, :])
    dsc = np.asarray(inputs["L0_ffn_down_scales"])
    segs = [(core * I_SH + 64 * s_) // 128 for s_ in range(11)]
    dns = np.ascontiguousarray(dsc[segs, :].T)
    rmsw = np.ascontiguousarray(
        np.asarray(inputs["L0_attn_rms_w"], dtype=np.float32).reshape(16, 128).T)
    def _wlay(w):
        return np.ascontiguousarray(
            np.asarray(w).reshape(16, 128, 16, 128).transpose(1, 2, 0, 3)
            .reshape(128, 16 * 2048))

    def _wolay(w):
        return np.ascontiguousarray(
            np.asarray(w).reshape(16, 128, 2048).transpose(1, 0, 2)
            .reshape(128, 16 * 2048))

    return {
        "x": x,
        "wq": _wlay(inputs["L0_attn_Wq"]),
        "wk": _wlay(inputs["L0_attn_Wk"]),
        "wv": _wlay(inputs["L0_attn_Wv"]),
        "wo": _wolay(inputs["L0_attn_Wo"]),
        "rmsw": rmsw,
        "gup": np.ascontiguousarray(
            np.asarray(inputs["L0_ffn_gate_up_packed"])[core * GU_SH:(core + 1) * GU_SH, :]),
        "gus": gus_t,
        "dnp": np.ascontiguousarray(
            np.asarray(inputs["L0_ffn_down_packed"])[:, core * WD_SH:(core + 1) * WD_SH]),
        "dns": dns,
        "c2": hc["c2"], "s2": hc["s2"], "dmT": hc["dmT"], "qdb": hc["qdb"],
        "kdt": hc["kdt"], "wcomb": hc["wcomb"], "ident": hc["ident"],
        "onesc": hc["onesc"], "onesr": hc["onesr"],
    }


def kernel(**inputs):
    nc, _ = build(debug=False)
    in_maps = [_mk_inmap(c, inputs) for c in range(NC)]
    res = run_bass_kernel_spmd(nc, in_maps, core_ids=list(range(NC)))
    out = np.zeros((B, T, E), dtype=np.float32)
    for c in range(NC):
        b, s = c // GRP, c % GRP
        out[b, s * TOK:(s + 1) * TOK, :] = res.results[c]["out"]
    return out
